# version 09
# baseline (speedup 1.0000x reference)
import contextlib
import numpy as np
import ml_dtypes
import concourse.bass as bass
import concourse.mybir as mybir
from concourse.bass_utils import run_bass_kernel_spmd
from concourse.aot_env import allow_ici_all_to_all

F32 = mybir.dt.float32
BF16 = mybir.dt.bfloat16
AF = mybir.ActivationFunctionType
ALU = mybir.AluOpType

NCORES = 8
B = 2
L = 4096
NTOK = B * L
DM = 2048
F = 8192
MYTOK = NTOK // NCORES
TC = 512
NCH = L // TC
FT = F // 128
EPS_LN = 1e-5
EPS_AT = 1e-6
LB_MIN = float(np.log(np.float64(0.8)))
LB_MAX = float(np.log(np.float64(0.9995)))
P2_TILES = 5
NEUTER_YE = False
PIPELINE_P3 = False


def _flat(waits):
    o = []
    for w in waits:
        if w is None:
            continue
        if isinstance(w, list):
            o.extend(x for x in w if x is not None)
        else:
            o.append(w)
    return o


class Prog:
    ENGS = ("T", "V", "A", "P", "Y")

    def __init__(self):
        self.ops = []
        self.counts = {}
        self.last = {}
        self.marks = {}
        self.limit = None
        self.meta = []

    def mark(self, name):
        self.marks[name] = len(self.ops)

    @staticmethod
    def _amt(sem):
        return 16 if sem.startswith("dma") else 1

    def op(self, eng, emit, waits=(), inc=None, pe_w=(), va_r=()):
        waits = tuple(_flat(waits))
        tok = None
        if inc is not None:
            self.counts[inc] = self.counts.get(inc, 0) + self._amt(inc)
            tok = (inc, self.counts[inc])
            self.last[eng] = tok
        self.ops.append((eng, emit, waits, tok))
        self.meta.append((tuple(pe_w), tuple(va_r)))
        return tok

    def snapshot(self):
        return [self.last[e] for e in self.ENGS if e in self.last]

    def emit(self, eng_name, eng_obj, sems):
        hi = {}
        ops = self.ops if self.limit is None else self.ops[:self.limit]
        for eng, emit, waits, tok in ops:
            if eng != eng_name:
                continue
            need = {}
            for (s, v) in waits:
                if need.get(s, -1) < v:
                    need[s] = v
            for s, v in need.items():
                if hi.get(s, -1) < v:
                    eng_obj.wait_ge(sems[s], v)
                    hi[s] = v
            ins = emit(eng_obj)
            if tok is not None:
                ins.then_inc(sems[tok[0]], self._amt(tok[0]))


def build(trunc=None):
    nc = bass.Bass("TRN2", target_bir_lowering=False)

    x_own = nc.dram_tensor("x_own", [MYTOK, DM], F32, kind="ExternalInput").ap()
    wqkv_h = nc.dram_tensor("wqkv_t", [128, 16 * 770], F32, kind="ExternalInput").ap()
    bg2 = nc.dram_tensor("bg2", [2], F32, kind="ExternalInput").ap()
    wo_h = nc.dram_tensor("wo", [DM, DM], F32, kind="ExternalInput").ap()
    bo_h = nc.dram_tensor("bo_t", [128, 16], F32, kind="ExternalInput").ap()
    wg_h = nc.dram_tensor("wg_t3", [FT, 128, 16 * 128], F32, kind="ExternalInput").ap()
    wu_h = nc.dram_tensor("wu_t3", [FT, 128, 16 * 128], F32, kind="ExternalInput").ap()
    wd_h = nc.dram_tensor("wd_t3", [16, 128, FT * 128], F32, kind="ExternalInput").ap()
    cos_h = nc.dram_tensor("cos_t", [L, 64], F32, kind="ExternalInput").ap()
    sin2_h = nc.dram_tensor("sin2_t", [L, 128], F32, kind="ExternalInput").ap()
    tri_h = nc.dram_tensor("tri_c", [128, 128], F32, kind="ExternalInput").ap()
    id_h = nc.dram_tensor("ident_c", [128, 128], F32, kind="ExternalInput").ap()
    ones_h = nc.dram_tensor("ones_c", [128, 2], F32, kind="ExternalInput").ap()

    out = nc.dram_tensor("out", [MYTOK, DM], F32, kind="ExternalOutput").ap()

    xn_loc = nc.dram_tensor("xn_loc", [DM, MYTOK], BF16)
    xn_full_h = nc.dram_tensor("xn_full", [NCORES * DM, MYTOK], BF16, addr_space="Shared")
    xn_full = xn_full_h.ap()
    y_loc_h = nc.dram_tensor("y_loc", [NCORES * 256, MYTOK], BF16)
    y_loc = y_loc_h.ap()
    y_t2_h = nc.dram_tensor("y_t2", [NCORES * 256, MYTOK], BF16)
    y_t2 = y_t2_h.ap()
    ffn_tok = nc.dram_tensor("ffn_tok", [MYTOK, DM], F32)

    p = Prog()
    root = contextlib.ExitStack()

    def sbuf(name, shape, dtype, st=None, side=None):
        kw = {"side": side} if side else {}
        return (st or root).enter_context(nc.sbuf_tensor(name, shape, dtype, **kw))

    ps_qk = root.enter_context(nc.psum_tensor("ps_qk", [128, 1024], F32))
    ps_vb = root.enter_context(nc.psum_tensor("ps_vb", [128, 1024], F32))
    ps_sc = root.enter_context(nc.psum_tensor("ps_sc", [128, 1024], F32))
    ps_ds = root.enter_context(nc.psum_tensor("ps_ds", [128, 512], F32))
    ps_t = root.enter_context(nc.psum_tensor("ps_t", [128, 1024], BF16))

    tri_sb = sbuf("tri_sb", [128, 128], F32)
    id_bf = sbuf("id_bf", [128, 128], BF16)
    ones_bf = sbuf("ones_bf", [128, 2], BF16)
    ones_f32 = sbuf("ones_f32", [128, 1], F32)
    ones_row = sbuf("ones_row", [1, 128], F32)
    bo_sb = sbuf("bo_sb", [128, 16], F32)
    nbg_sb = sbuf("nbg_sb", [128, 2], F32)
    bg_row = sbuf("bg_row", [1, 2], F32)
    gated_a = sbuf("gated_a", [128, P2_TILES, MYTOK], BF16)
    pXN = contextlib.ExitStack()
    xnT_own = sbuf("xnT_own", [128, 16, MYTOK], BF16, pXN, side="right")
    wgu_sb = sbuf("wgu_sb", [128, 2, 2, 16, 128], BF16, pXN, side="right")
    sg_sb = sbuf("sg_sb", [128, 1024], F32, pXN, side="right")
    pA = contextlib.ExitStack()
    wqkv_sb = sbuf("wqkv_sb", [128, 16, 770], BF16, pA)

    t_tri = p.op("Y", lambda e: e.dma_start(out=tri_sb[:], in_=tri_h), inc="dma_tri")
    t_bo = p.op("Y", lambda e: e.dma_start(out=bo_sb[:], in_=bo_h), inc="dma_bo")
    t_idb = p.op("P", lambda e: e.dma_start(out=id_bf[:], in_=id_h), inc="dma_idb")
    t_one = p.op("P", lambda e: e.dma_start(out=ones_bf[:], in_=ones_h), inc="dma_one")
    t_o1 = p.op("V", lambda e: e.memset(ones_f32[:], 1.0), inc="V")
    t_or = p.op("V", lambda e: e.memset(ones_row[:], 1.0), inc="V")
    t_bg = p.op("Y", lambda e: e.dma_start(out=bg_row[:], in_=bg2.unsqueeze(0)),
                inc="dma_bg")
    t_pb = p.op("T", lambda e: e.matmul(ps_ds[:, 0:2], ones_row[:], bg_row[:],
                                        start=True, stop=True), waits=[t_bg, t_or], inc="T", pe_w=(6,))
    t_nbg = p.op("V", lambda e: e.tensor_scalar(nbg_sb[:], ps_ds[:, 0:2], -1.0, None, ALU.mult),
                 waits=[t_pb], inc="V", va_r=(6,))
    t_wqkv = p.op("P", lambda e: e.dma_start(
        out=wqkv_sb[:].rearrange("p a b -> p (a b)"), in_=wqkv_h), inc="dma_qkv")

    pP1 = contextlib.ExitStack()
    xf = sbuf("xf", [128, 2, DM], F32, pP1)
    st6 = sbuf("st6", [128, 24], F32, pP1)
    agg = sbuf("agg", [128, 2], F32, pP1)
    rstd = sbuf("rstd", [128, 1], F32, pP1)
    nmr = sbuf("nmr", [128, 1], F32, pP1)
    xn_tok = sbuf("xn_tok", [128, 2, DM], BF16, pP1)

    r_xf = [None, None]
    r_tpchain = None
    prev_apply = None
    for t in range(8):
        pb = t % 2
        tok_ld = p.op("Y", lambda e, t=t, pb=pb: e.dma_start(
            out=xf[:, pb], in_=x_own[t * 128:(t + 1) * 128, :]), waits=[r_xf[pb]],
            inc="dma_x%d" % pb)

        def stats(e, pb=pb):
            for i in range(4):
                ins = e.bn_stats(st6[:, i * 6:(i + 1) * 6], xf[:, pb, i * 512:(i + 1) * 512])
            return ins
        tok_st = p.op("V", stats, waits=[tok_ld, prev_apply], inc="V")
        tok_ag = p.op("V", lambda e: e.bn_aggr(agg[:], st6[:]), waits=[tok_st], inc="V")
        tok_r0 = p.op("V", lambda e: e.tensor_scalar(rstd[:], agg[:, 1:2], EPS_LN, None, ALU.add),
                      waits=[tok_ag], inc="V")
        tok_r1 = p.op("A", lambda e: e.activation(rstd[:], rstd[:], AF.Ln), waits=[tok_r0], inc="A")
        tok_r2 = p.op("A", lambda e: e.activation(rstd[:], rstd[:], AF.Exp, scale=-0.5),
                      waits=[tok_r1], inc="A")
        tok_nm = p.op("V", lambda e: e.tensor_scalar(nmr[:], agg[:, 0:1], rstd[:], -1.0,
                                                     ALU.mult, ALU.mult), waits=[tok_r2], inc="V")
        tok_ap = p.op("A", lambda e, pb=pb: e.activation(
            xn_tok[:, pb], xf[:, pb], AF.Identity, bias=nmr[:], scale=rstd[:]),
            waits=[tok_nm], inc="A")
        prev_apply = tok_ap
        r_xf[pb] = tok_ap

        for g in range(2):
            def tr(e, pb=pb, g=g):
                for kk in range(8):
                    ins = e.transpose(ps_t[:, kk * 128:(kk + 1) * 128],
                                      xn_tok[:, pb, (g * 8 + kk) * 128:(g * 8 + kk + 1) * 128],
                                      id_bf[:])
                return ins
            tok_tr = p.op("T", tr, waits=[tok_ap, r_tpchain, t_idb], inc="T", pe_w=(7,))

            def ev(e, t=t, g=g):
                return e.tensor_copy(
                    xnT_own[:, g * 8:(g + 1) * 8, t * 128:(t + 1) * 128],
                    ps_t[:].rearrange("p (a b) -> p a b", a=8, b=128))
            r_tpchain = p.op("V", ev, waits=[tok_tr], inc="V", va_r=(7,))

    xn_done = [r_tpchain]
    ag_in = []
    for k in range(16):
        ag_in.append(p.op("Y", lambda e, k=k: e.dma_start(
            out=xn_loc[k * 128:(k + 1) * 128, :], in_=xnT_own[:, k, :]),
            waits=xn_done, inc="dma_xnl"))
    p.op("P", lambda e: e.collective_compute(
        "AllGather", ALU.bypass, replica_groups=[list(range(NCORES))],
        ins=[xn_loc.ap().opt()], outs=[xn_full_h.ap().opt()]), waits=ag_in, inc="cc")

    st_ffn = {"wgu": [None, None], "sg": None, "up_ps": None}

    def ffn_a_tile(ff, gated_dst, extra_waits=()):
        dbuf = ff % 2
        w1 = p.op("P", lambda e, ff=ff, dbuf=dbuf: e.dma_start(
            out=wgu_sb[:, 0, dbuf].rearrange("p a b -> p (a b)"), in_=wg_h[ff]),
            waits=[st_ffn["wgu"][dbuf]] + list(extra_waits), inc="dma_wg%d" % dbuf)
        w2 = p.op("P", lambda e, ff=ff, dbuf=dbuf: e.dma_start(
            out=wgu_sb[:, 1, dbuf].rearrange("p a b -> p (a b)"), in_=wu_h[ff]),
            waits=[st_ffn["wgu"][dbuf]], inc="dma_wu%d" % dbuf)

        def mm_gate(e, dbuf=dbuf):
            for k in range(16):
                for h2 in range(2):
                    ins = e.matmul(ps_qk[:, h2 * 512:(h2 + 1) * 512], wgu_sb[:, 0, dbuf, k],
                                   xnT_own[:, k, h2 * 512:(h2 + 1) * 512],
                                   start=(k == 0), stop=(k == 15))
            return ins
        tg = p.op("T", mm_gate,
                  waits=[w1, st_ffn["sg"]] + xn_done + list(extra_waits), inc="T", pe_w=(0, 1))

        def mm_up(e, dbuf=dbuf):
            for k in range(16):
                for h2 in range(2):
                    ins = e.matmul(ps_vb[:, h2 * 512:(h2 + 1) * 512], wgu_sb[:, 1, dbuf, k],
                                   xnT_own[:, k, h2 * 512:(h2 + 1) * 512],
                                   start=(k == 0), stop=(k == 15))
            return ins
        tu = p.op("T", mm_up, waits=[w2, st_ffn["up_ps"]] + list(extra_waits), inc="T", pe_w=(2, 3))

        tsg = p.op("A", lambda e: e.activation(sg_sb[:], ps_qk[:], AF.Silu),
                   waits=[tg, st_ffn["up_ps"]], inc="A", va_r=(0, 1))
        st_ffn["sg"] = tsg

        tgm = p.op("V", lambda e, gated_dst=gated_dst: e.tensor_tensor(
            gated_dst, sg_sb[:], ps_vb[:], ALU.mult), waits=[tsg, tu], inc="V", va_r=(2, 3))
        st_ffn["up_ps"] = tgm
        st_ffn["wgu"][dbuf] = tgm
        return tgm

    gated_toks = []
    for ff in range(P2_TILES):
        gated_toks.append(ffn_a_tile(ff, gated_a[:, ff, :]))

    pP1.close()
    xnT_st = sbuf("xnT_st", [128, 2, 2, 16, TC], BF16, pA)
    ropeC = sbuf("ropeC", [128, 4, 64], F32, pA)
    ropeS = sbuf("ropeS", [128, 4, 128], F32, pA)
    qkf = sbuf("qkf", [128, 8, 512], F32, pA)
    phis = sbuf("phis", [128, 2, 512], F32, pA)
    qkt = sbuf("qkt", [128, 8, 512], BF16, pA)
    qkfT = sbuf("qkfT", [128, 8, 512], BF16, pA)
    vsb = sbuf("vsb", [128, 8, 2, 132], BF16, pA)
    aw_sb = sbuf("aw_sb", [128, 4, 128], BF16, pA)
    ssb = sbuf("ssb", [128, 4, 132], F32, pA)
    ssb_bf = sbuf("ssb_bf", [128, 4, 132], BF16, pA)
    y_tokb = sbuf("y_tokb", [128, 16, 128], BF16, pA)
    yT_sb = sbuf("yT_sb", [128, 2, 2, 512], BF16, pA)
    bsb = sbuf("bsb", [128, 2, 8], F32, pA)
    ebs = sbuf("ebs", [128, 2, 8], F32, pA)
    lb_sb = sbuf("lb_sb", [128, 2, 8], F32, pA)
    psc_sb = sbuf("psc_sb", [128, 4, 8], F32, pA)
    plr_sb = sbuf("plr_sb", [1, 16], F32, pA)
    plast_sb = sbuf("plast_sb", [128, 2, 8], F32, pA)
    den_sb = sbuf("den_sb", [128, 2, 1], F32, pA)

    t_vones = p.op("V", lambda e: e.memset(
        vsb[:].rearrange("p a b c -> p (a b c)"), 1.0), inc="V")

    S = dict(
        r_xnT=[None, None], r_rope=None,
        r_qk=[st_ffn["sg"], st_ffn["sg"]], r_vb=[st_ffn["up_ps"], st_ffn["up_ps"]],
        r_sc=[[], []], r_dsl=[[t_vones], [t_vones]], r_clog=[t_nbg], r_psc=[],
        r_Aslot=[None, None], r_yslot=[[], []],
        r_tpchain=xn_done[0], r_qkf=None, r_phis=None,
        r_qkt=None, r_qkfT=None, r_vsb=t_vones,
        r_aw=[None, None, None, None], r_ytok=None, r_yT=None,
        r_plr=None, r_plbc=t_nbg, r_den=[None, None], r_bsb1=None,
        state_tok=[None, None, None, None], state_bf_tok=[None, None, None, None],
        g_cnt=0,
    )
    y_stores = []

    def p3_proj(ci, gi):
        ctx = {"sl": gi % 2, "ci": ci, "gi": gi, "psum_reads": []}
        sl = gi % 2
        l0 = ci * TC
        w_first = [("cc", 1)] if gi == 0 else []
        xn_toks = []
        for b in range(2):
            t0 = b * L + ci * TC
            rank, col0 = divmod(t0, MYTOK)
            for kq in range(2):
                xn_toks.append(p.op("Y", lambda e, rank=rank, col0=col0, sl=sl, b=b, kq=kq:
                    e.dma_start(
                        out=xnT_st[:, sl, b, kq * 8:(kq + 1) * 8],
                        in_=xn_full[rank * DM + kq * 1024: rank * DM + (kq + 1) * 1024,
                                    col0:col0 + TC].rearrange("(k p) n -> p k n", p=128)),
                    waits=[S["r_xnT"][sl]] + w_first, inc="dma_xn%d" % sl))
        tok_rc = p.op("Y", lambda e, l0=l0: e.dma_start(
            out=ropeC[:],
            in_=cos_h[l0:l0 + TC, :].rearrange("(s p) f -> p s f", p=128)),
            waits=[S["r_rope"]], inc="dma_rc0")
        tok_rs2 = p.op("Y", lambda e, l0=l0: e.dma_start(
            out=ropeS[:],
            in_=sin2_h[l0:l0 + TC, :].rearrange("(s p) f -> p s f", p=128)),
            waits=[S["r_rope"]], inc="dma_rs0")

        ctx.update(xn_toks=xn_toks, tok_rc=tok_rc, tok_rs2=tok_rs2)
        vb_toks = []
        for s8p in range(4):
            def mm_vb2(e, s8p=s8p, sl=sl):
                for j in range(2):
                    s8 = s8p * 2 + j
                    b, s = divmod(s8, 4)
                    for k in range(16):
                        ins = e.matmul(ps_vb[:, j * 512:j * 512 + 258],
                                       xnT_st[:, sl, b, k, s * 128:(s + 1) * 128],
                                       wqkv_sb[:, k, 512:770], start=(k == 0), stop=(k == 15))
                return ins
            t_mm = p.op("T", mm_vb2, waits=xn_toks + [t_wqkv, S["r_vb"][0], S["r_vb"][1],
                        S["r_plbc"]], inc="T", pe_w=(2, 3))

            def ev_v2(e, s8p=s8p):
                for j in range(2):
                    ins = e.copy(vsb[:, s8p * 2 + j, :, 0:128],
                                 ps_vb[:, j * 512:j * 512 + 256]
                                 .rearrange("p (h d) -> p h d", h=2, d=128))
                return ins
            t_evv = p.op("A", ev_v2, waits=[t_mm, S["r_vsb"]], inc="A", va_r=(2, 3))

            def cb2(e, s8p=s8p):
                for j in range(2):
                    ins = e.tensor_copy(bsb[:, :, s8p * 2 + j],
                                        ps_vb[:, j * 512 + 256:j * 512 + 258])
                return ins
            t_cb = p.op("V", cb2, waits=[t_mm, S["r_bsb1"]], inc="V", va_r=(2, 3))
            S["r_vb"][0] = [t_evv, t_cb]
            S["r_vb"][1] = [t_evv, t_cb]
            vb_toks.extend([[t_evv, t_cb], [t_evv, t_cb]])
            ctx["psum_reads"].extend([t_evv, t_cb])
        ctx["vb_toks"] = vb_toks

        return ctx

    def p3_mid(ctx):
        sl = ctx["sl"]
        w_exp = _flat(ctx["vb_toks"]) + [t_nbg]
        eb_toks = []
        for h in range(2):
            eb_toks.append(p.op("A", lambda e, h=h, sl=sl: e.activation(
                ebs[:, h, :], bsb[:, h, :], AF.Exp, scale=-1.0, bias=nbg_sb[:, h:h + 1]),
                waits=w_exp + [S["r_ebs"]] if "r_ebs" in S else w_exp, inc="A"))
        S["r_bsb1"] = eb_toks
        t_t2 = p.op("V", lambda e: e.tensor_scalar(
            ebs[:].rearrange("p a b -> p (a b)"), ebs[:].rearrange("p a b -> p (a b)"),
            1.0, None, ALU.add), waits=eb_toks, inc="V")
        t_ln = p.op("A", lambda e: e.activation(
            lb_sb[:].rearrange("p a b -> p (a b)"), ebs[:].rearrange("p a b -> p (a b)"),
            AF.Ln), waits=[t_t2] + S.get("r_lb", []), inc="A")
        S["r_ebs"] = t_ln
        t_lb1 = p.op("V", lambda e: e.tensor_scalar(
            lb_sb[:].rearrange("p a b -> p (a b)"), lb_sb[:].rearrange("p a b -> p (a b)"),
            -1.0, LB_MAX, ALU.mult, ALU.min), waits=[t_ln], inc="V")
        t_lb = p.op("V", lambda e: e.tensor_scalar(
            lb_sb[:].rearrange("p a b -> p (a b)"), lb_sb[:].rearrange("p a b -> p (a b)"),
            LB_MIN, None, ALU.max), waits=[t_lb1], inc="V")

        t_clog = p.op("T", lambda e: e.matmul(
            ps_ds[:, 264:280], tri_sb[:], lb_sb[:].rearrange("p a b -> p (a b)"),
            start=True, stop=True),
            waits=[t_lb, t_tri] + S["r_clog"] + S["r_dsl"][0] + S["r_dsl"][1], inc="T",
            pe_w=(6,))
        t_sum = p.op("T", lambda e: e.matmul(
            ps_ds[0:1, 280:296], ones_f32[:], lb_sb[:].rearrange("p a b -> p (a b)"),
            start=True, stop=True), waits=[t_lb, t_o1], inc="T", pe_w=(6,))
        S["r_lb"] = [t_clog, t_sum]
        t_p1 = p.op("A", lambda e: e.activation(
            psc_sb[:, 0:2, :].rearrange("p a b -> p (a b)"), ps_ds[:, 264:280], AF.Exp),
            waits=[t_clog, t_sum] + S["r_psc"], inc="A", va_r=(6,))
        t_p2 = p.op("A", lambda e: e.activation(
            psc_sb[:, 2:4, :].rearrange("p a b -> p (a b)"), ps_ds[:, 264:280], AF.Exp,
            scale=-1.0), waits=[t_clog, t_sum] + S["r_psc"], inc="A", va_r=(6,))
        t_pe = p.op("V", lambda e: e.tensor_copy(plr_sb[:], ps_ds[0:1, 280:296]),
                    waits=[t_sum, S["r_plr"]], inc="V", va_r=(6,))
        t_pbc = p.op("T", lambda e: e.matmul(
            ps_ds[:, 296:312], ones_row[:], plr_sb[:], start=True, stop=True),
            waits=[t_pe, t_p1, t_p2, S["r_plbc"]], inc="T", pe_w=(6,))
        S["r_plr"] = t_pbc
        t_pl = p.op("A", lambda e: e.activation(
            plast_sb[:].rearrange("p a b -> p (a b)"), ps_ds[:, 296:312], AF.Exp),
            waits=[t_pbc] + S["state_tok"], inc="A", va_r=(6,))
        S["r_plbc"] = t_pl
        S["r_clog"] = [t_pl, t_pe]
        ctx.update(t_p1=t_p1, t_p2=t_p2, t_pe=t_pe, t_pl=t_pl)
        return ctx

    def p3_qk(ctx):
        sl = ctx["sl"]
        xn_toks = ctx["xn_toks"]
        tok_rc = ctx["tok_rc"]
        tok_rs2 = ctx["tok_rs2"]
        ctx["f5s"] = {}
        for s8p in range(4):
            def mm_qk2(e, s8p=s8p, sl=sl):
                for j in range(2):
                    s8 = s8p * 2 + j
                    b, s = divmod(s8, 4)
                    for k in range(16):
                        ins = e.matmul(ps_qk[:, j * 512:(j + 1) * 512],
                                       xnT_st[:, sl, b, k, s * 128:(s + 1) * 128],
                                       wqkv_sb[:, k, 0:512], start=(k == 0), stop=(k == 15))
                return ins
            t_mmqk = p.op("T", mm_qk2, waits=xn_toks + [t_wqkv, S["r_qk"][0], S["r_qk"][1]],
                          inc="T", pe_w=(0, 1))
            S["r_xnT"][sl] = t_mmqk

            for j in range(2):
                s8 = s8p * 2 + j
                b, s = divmod(s8, 4)
                pb = j

                def rope1(e, s=s, pb=pb):
                    src2 = ps_qk[:, pb * 512:(pb + 1) * 512].rearrange(
                        "p (g two f) -> p g two f", g=4, two=2, f=64)
                    sw = src2[:, :, ::-1, :]
                    sinb = ropeS[:, s].rearrange("p (two f) -> p two f", two=2) \
                        .unsqueeze(1).broadcast_to((128, 4, 2, 64))
                    return e.tensor_tensor(
                        phis[:, 0].rearrange("p (g two f) -> p g two f", g=4, two=2, f=64),
                        sw, sinb, ALU.mult)
                t_r1 = p.op("V", rope1, waits=[t_mmqk, tok_rs2, S["r_phis"]], inc="V",
                            va_r=(pb,))

                def rope2(e, s=s, pb=pb):
                    src2 = ps_qk[:, pb * 512:(pb + 1) * 512].rearrange(
                        "p (g two f) -> p g two f", g=4, two=2, f=64)
                    cosb = ropeC[:, s].unsqueeze(1).unsqueeze(2).broadcast_to((128, 4, 2, 64))
                    return e.tensor_tensor(
                        phis[:, 1].rearrange("p (g two f) -> p g two f", g=4, two=2, f=64),
                        src2, cosb, ALU.mult)
                t_r2 = p.op("V", rope2, waits=[t_mmqk, tok_rc, S["r_phis"]], inc="V",
                            va_r=(pb,))
                S["r_qk"][pb] = [t_r1, t_r2]
                ctx["psum_reads"].extend([t_r1, t_r2])
                S["r_rope"] = t_r2

                t_r3 = p.op("V", lambda e, s8=s8: e.tensor_tensor(
                    qkf[:, s8], phis[:, 0], phis[:, 1], ALU.add),
                    waits=[t_r1, t_r2, S["r_qkf"]], inc="V")
                S["r_phis"] = t_r3

                t_f1 = p.op("V", lambda e, s8=s8: e.tensor_scalar(
                    phis[:, 0], qkf[:, s8], 0.0, None, ALU.min),
                    waits=[t_r3, S["r_phis"]], inc="V")
                t_f2 = p.op("A", lambda e: e.activation(phis[:, 1], phis[:, 0], AF.Exp),
                            waits=[t_f1], inc="A")

                def f3(e, s8=s8):
                    pv = psc_sb[:, :, s8].unsqueeze(2).broadcast_to((128, 4, 128))
                    return e.scalar_tensor_tensor(
                        phis[:, 0].rearrange("p (g f) -> p g f", g=4, f=128),
                        qkf[:, s8].rearrange("p (g f) -> p g f", g=4, f=128),
                        1.0, pv, ALU.add, ALU.mult)
                t_f3 = p.op("V", f3, waits=[t_f1, t_f2, ctx["t_p1"], ctx["t_p2"]], inc="V")
                S["r_qkf"] = t_f3

                def f4(e, s8=s8):
                    pv = psc_sb[:, :, s8].unsqueeze(2).broadcast_to((128, 4, 128))
                    return e.tensor_tensor(
                        phis[:, 1].rearrange("p (g f) -> p g f", g=4, f=128),
                        phis[:, 1].rearrange("p (g f) -> p g f", g=4, f=128),
                        pv, ALU.mult)
                t_f4 = p.op("V", f4, waits=[t_f2, ctx["t_p1"], ctx["t_p2"]], inc="V")

                t_f5 = p.op("V", lambda e, s8=s8: e.tensor_tensor(
                    qkt[:, s8], phis[:, 0], phis[:, 1], ALU.max),
                    waits=[t_f3, t_f4, S["r_qkt"]], inc="V")
                S["r_phis"] = t_f5
                ctx["f5s"][s8] = t_f5
                ctx["t_f34l"] = [t_f3, t_f4]
        return ctx


    def p3_scan(ctx, guard=()):
        sl = ctx["sl"]
        ci = ctx["ci"]
        vb_toks = ctx["vb_toks"]
        t_p1, t_p2, t_pe, t_pl = ctx["t_p1"], ctx["t_p2"], ctx["t_pe"], ctx["t_pl"]
        if ci == 0:
            t_rs = p.op("V", lambda e: e.memset(ssb[:].rearrange("p a b -> p (a b)"), 0.0),
                        waits=S["state_tok"] + S["state_bf_tok"], inc="V")
            t_rsb = p.op("V", lambda e: e.memset(ssb_bf[:].rearrange("p a b -> p (a b)"), 0.0),
                         waits=S["state_tok"] + S["state_bf_tok"], inc="V")
            S["state_tok"] = [t_rs] * 4
            S["state_bf_tok"] = [t_rsb] * 4

        ye_toks = {0: [], 1: []}
        t_ds_last = None
        order = [0, 4, 1, 5, 2, 6, 3, 7]
        for oi, s8 in enumerate(order):
            b, s = divmod(s8, 4)
            t_f5 = ctx["f5s"][s8]

            def trq(e, sl=sl, s8=s8):
                for g in range(4):
                    ins = e.transpose(ps_t[:, g * 128:(g + 1) * 128],
                                      qkt[:, s8, g * 128:(g + 1) * 128], id_bf[:])
                return ins
            t_tr = p.op("T", trq, waits=[t_f5, S["r_tpchain"], t_idb] +
                        list(guard if oi == 0 else ()), inc="T", pe_w=(7,))
            t_evq = p.op("V", lambda e, s8=s8: e.tensor_copy(
                qkfT[:, s8], ps_t[:, 0:512]),
                waits=[t_tr, S["r_qkfT"]], inc="V", va_r=(7,))
            S["r_tpchain"] = t_evq

            g0 = S["g_cnt"]
            gp = [g0 % 2, (g0 + 1) % 2]
            chains = [b * 2, b * 2 + 1]
            a_aps = [ps_sc[:, gp[h] * 512: gp[h] * 512 + 128] for h in range(2)]
            y_aps = [ps_sc[:, gp[h] * 512 + 128: gp[h] * 512 + 257] for h in range(2)]
            ds_aps = [ps_ds[:, gp[h] * 132: gp[h] * 132 + 129] for h in range(2)]
            qfs = [qkfT[:, s8, h * 128:(h + 1) * 128] for h in range(2)]
            kfs = [qkfT[:, s8, 256 + h * 128:256 + h * 128 + 128] for h in range(2)]
            kt = qkt[:, s8, 256:512]
            vv = vsb[:, s8]
            awp = [g0 % 4, (g0 + 1) % 4]

            def mm_A2(e, a_aps=a_aps, kfs=kfs, qfs=qfs):
                for h in range(2):
                    ins = e.matmul(a_aps[h], kfs[h], qfs[h], start=True, stop=True)
                return ins
            t_A2 = p.op("T", mm_A2, waits=[t_evq, S["r_Aslot"][0], S["r_Aslot"][1]] +
                        S["r_yslot"][0] + S["r_yslot"][1], inc="T", pe_w=(4, 5))

            aw_toks = []
            for h in range(2):
                aw_toks.append(p.op("V", lambda e, h=h, a_aps=a_aps, awp=awp: e.tensor_tensor(
                    aw_sb[:, awp[h]], a_aps[h], tri_sb[:], ALU.mult),
                    waits=[t_A2, S["r_aw"][awp[h]], t_tri], inc="V", va_r=(4 + gp[h],)))
                S["r_Aslot"][gp[h]] = aw_toks[h]

            def mm_Y2(e, y_aps=y_aps, ds_aps=ds_aps, awp=awp, vv=vv, qfs=qfs, kt=kt,
                      chains=chains):
                for h in range(2):
                    e.matmul(y_aps[h], aw_sb[:, awp[h]], vv[:, h, 0:129],
                             start=True, stop=False)
                    e.matmul(y_aps[h], qfs[h], ssb_bf[:, chains[h], 0:129],
                             start=False, stop=True)
                    ins = e.matmul(ds_aps[h], kt[:, h * 128:(h + 1) * 128], vv[:, h, 0:129],
                                   start=True, stop=True)
                return ins
            t_Y2 = p.op("T", mm_Y2, waits=aw_toks +
                        [S["state_bf_tok"][chains[0]], S["state_bf_tok"][chains[1]],
                         t_f5, t_p1, t_p2, t_pe, t_pl] + vb_toks[s8] +
                        S["r_dsl"][0] + S["r_dsl"][1], inc="T", pe_w=(4, 5, 6))
            t_ds_last = t_Y2
            S["r_aw"][awp[0]] = t_Y2
            S["r_aw"][awp[1]] = t_Y2

            for h in range(2):
                chain = chains[h]
                t_de = p.op("V", lambda e, h=h, y_aps=y_aps: e.tensor_scalar(
                    den_sb[:, gp[h]], y_aps[h][:, 128:129], EPS_AT, None, ALU.add),
                    waits=[t_Y2, S["r_den"][gp[h]]], inc="V", va_r=(4 + gp[h],))
                t_dr = p.op("V", lambda e, h=h: e.reciprocal(
                    den_sb[:, gp[h]], den_sb[:, gp[h]]), waits=[t_de], inc="V")
                yslot = s8 * 2 + h
                t_ye = p.op("A", lambda e, h=h, y_aps=y_aps, yslot=yslot: e.activation(
                    y_tokb[:, yslot], y_aps[h][:, 0:128], AF.Identity,
                    scale=den_sb[:, gp[h]]), waits=[t_dr, S["r_ytok"]], inc="A",
                    va_r=(4 + gp[h],))
                S["r_den"][gp[h]] = t_ye
                S["r_yslot"][gp[h]] = [t_de, t_ye]
                ye_toks[b].append(t_ye)

                t_s1 = p.op("V", lambda e, chain=chain, h=h, ds_aps=ds_aps: e.tensor_tensor(
                    ssb[:, chain, 0:129], ssb[:, chain, 0:129], ds_aps[h], ALU.add),
                    waits=[t_Y2, S["state_tok"][chain]], inc="V", va_r=(6,))
                t_s2 = p.op("V", lambda e, chain=chain, h=h, s8=s8: e.tensor_scalar(
                    ssb[:, chain, 0:129], ssb[:, chain, 0:129], plast_sb[:, h, s8:s8 + 1],
                    None, ALU.mult), waits=[t_s1, t_pl], inc="V")
                S["state_tok"][chain] = t_s2
                S["r_dsl"][gp[h]] = [t_s1]
                t_sb2 = p.op("A", lambda e, chain=chain: e.copy(
                    ssb_bf[:, chain, 0:129], ssb[:, chain, 0:129]),
                    waits=[t_s2, t_Y2], inc="A")
                S["state_bf_tok"][chain] = t_sb2
            S["g_cnt"] += 2

        S["r_vsb"] = S["state_bf_tok"][3]
        S["r_qkt"] = S["state_bf_tok"][3]
        S["r_qkfT"] = S["state_bf_tok"][3]
        S["r_psc"] = ctx["t_f34l"] + [t_ds_last]

        for b in range(2):
            def tr_y(e, b=b):
                for j in range(8):
                    ins = e.transpose(ps_t[:, j * 128:(j + 1) * 128],
                                      y_tokb[:, b * 8 + j], id_bf[:])
                return ins
            t_yT = p.op("T", tr_y, waits=ye_toks[b] + [S["r_tpchain"]], inc="T", pe_w=(7,))

            def ev_y(e, b=b):
                src_v = ps_t[:].rearrange("p (s2 h d) -> p s2 h d", s2=4, h=2, d=128) \
                    .rearrange("p s2 h d -> p h s2 d")
                return e.tensor_copy(
                    yT_sb[:, b].rearrange("p h (s2 d) -> p h s2 d", s2=4, d=128), src_v)
            t_yTe = p.op("V", ev_y, waits=[t_yT, S["r_yT"]], inc="V", va_r=(7,))
            S["r_tpchain"] = t_yTe
            S["r_ytok"] = t_yT

            t0b = b * L + ci * TC
            rank, col0 = divmod(t0b, MYTOK)
            for h in range(2):
                y_stores.append(p.op("Y", lambda e, rank=rank, col0=col0, b=b, h=h:
                    e.dma_start(
                        out=y_loc[rank * 256 + h * 128: rank * 256 + (h + 1) * 128,
                                  col0:col0 + TC],
                        in_=yT_sb[:, b, h, :]), waits=[t_yTe], inc="dma_ys0"))
            S["r_yT"] = y_stores[-1]

    p.mark("preP3")
    for gi, ci in enumerate(range(NCH)):
        ctx = p3_proj(ci, gi)
        p3_mid(ctx)
        p3_qk(ctx)
        p3_scan(ctx)
        p.mark("chunk%d" % gi)
    p.mark("P3done")

    r_qk = S["r_qk"]
    r_vb = S["r_vb"]
    r_sc = S["r_sc"]
    r_dsl = S["r_dsl"]
    r_clog = S["r_clog"]

    p3_done = p.snapshot()
    p.op("P", lambda e: e.collective_compute(
        "AllToAll", ALU.bypass, replica_groups=[list(range(NCORES))],
        ins=[y_loc_h.ap().opt()], outs=[y_t2_h.ap().opt()]),
        waits=y_stores + [("cc", 1)], inc="cc")

    p.mark("A2A")
    p3_ps_free = _flat([r_qk[0], r_qk[1], r_vb[0], r_vb[1]])
    pA.close()

    pB = contextlib.ExitStack()
    gated_b = sbuf("gated_b", [128, FT - P2_TILES, MYTOK], BF16, pB)
    for ff in range(P2_TILES, FT):
        extra = (p3_ps_free + p3_done) if ff == P2_TILES else ()
        gated_toks.append(ffn_a_tile(ff, gated_b[:, ff - P2_TILES, :], extra))
    gated_all = gated_toks[-1]

    def gated_get(f):
        if f < P2_TILES:
            return gated_a[:, f, :]
        return gated_b[:, f - P2_TILES, :]

    p.mark("P4a")
    pXN.close()
    id_f32 = sbuf("id_f32", [128, 128], F32, pB)
    t_idf = p.op("Y", lambda e: e.dma_start(out=id_f32[:], in_=id_h),
                 waits=p3_done, inc="dma_idf")
    wd_sb = sbuf("wd_sb", [128, 2, FT, 128], BF16, pB)
    ffn_f = sbuf("ffn_f", [128, 2, 512], F32, pB)
    fftT = sbuf("fftT", [128, 2, 512], F32, pB)

    r_wd = [[gated_all] + p3_done, [gated_all] + p3_done]
    p3_sc_free = _flat([S["r_Aslot"], S["r_yslot"][0], S["r_yslot"][1]])
    r_dn = [p3_sc_free, p3_sc_free]
    r_ff = [None, None]
    r_fftT = [None, None]
    r_dstr = [r_dsl[0] + r_dsl[1] + r_clog, None]
    last_dn = None
    ffn_stores = []
    dn_cnt = 0
    for dout in range(16):
        dbuf = dout % 2
        t_wd = p.op("P", lambda e, dout=dout, dbuf=dbuf: e.dma_start(
            out=wd_sb[:, dbuf].rearrange("p a b -> p (a b)"), in_=wd_h[dout]),
            waits=[r_wd[dbuf]], inc="dma_wd%d" % dbuf)
        for half in range(2):
            par = dn_cnt % 2
            bank = ps_sc[:, par * 512:(par + 1) * 512]

            def mm_dn(e, dbuf=dbuf, half=half, bank=bank):
                for f in range(FT):
                    ins = e.matmul(bank[:, 0:512], wd_sb[:, dbuf, f],
                                   gated_get(f)[:, half * 512:(half + 1) * 512],
                                   start=(f == 0), stop=(f == FT - 1))
                return ins
            t_dn = p.op("T", mm_dn, waits=[t_wd, gated_all, r_dn[par]], inc="T", pe_w=(4 + par,))
            last_dn = t_dn

            t_fe = p.op("A", lambda e, par=par, bank=bank, dout=dout: e.activation(
                ffn_f[:, par], bank[:, 0:512], AF.Identity, bias=bo_sb[:, dout:dout + 1]),
                waits=[t_dn, r_ff[par], t_bo], inc="A", va_r=(4 + par,))
            r_dn[par] = t_fe

            def trf(e, par=par):
                for i in range(4):
                    ins = e.transpose(ps_ds[:, i * 128:(i + 1) * 128],
                                      ffn_f[:, par, i * 128:(i + 1) * 128], id_f32[:])
                return ins
            t_tf = p.op("T", trf, waits=[t_fe, r_dstr[0], r_dstr[1], t_idf], inc="T", pe_w=(6,))
            r_ff[par] = t_tf

            t_te = p.op("V", lambda e, par=par: e.tensor_copy(fftT[:, par], ps_ds[:, 0:512]),
                        waits=[t_tf, r_fftT[par]], inc="V", va_r=(6,))
            r_dstr = [t_te, None]

            t_sf = p.op("Y", lambda e, par=par, half=half, dout=dout: e.dma_start(
                out=ffn_tok[half * 512:(half + 1) * 512, dout * 128:(dout + 1) * 128]
                .rearrange("(i p) c -> p i c", p=128),
                in_=fftT[:, par].rearrange("p (i c) -> p i c", i=4, c=128)),
                waits=[t_te], inc="dma_ff%d" % par)
            ffn_stores.append(t_sf)
            r_fftT[par] = t_sf
            dn_cnt += 1
        r_wd[dbuf] = last_dn

    p.mark("P4b")
    p4_done = p.snapshot()
    pB.close()
    pC = contextlib.ExitStack()
    ysb = sbuf("ysb", [128, 16, MYTOK], BF16, pC)
    wo_sb = sbuf("wo_sb", [128, 16, DM], BF16, pC)
    xr = sbuf("xr", [128, 2, DM], F32, pC)
    ffsb = sbuf("ffsb", [128, 2, DM], F32, pC)
    outsb = sbuf("outsb", [128, 2, DM], F32, pC)

    y_ld = []
    for k in range(16):
        y_ld.append(p.op("Y", lambda e, k=k: e.dma_start(
            out=ysb[:, k, :], in_=y_t2[k * 128:(k + 1) * 128, :]),
            waits=[("cc", 2)] + p4_done, inc="dma_y2"))
    wo_ld = []
    for k in range(16):
        wo_ld.append(p.op("P", lambda e, k=k: e.dma_start(
            out=wo_sb[:, k, :], in_=wo_h[k * 128:(k + 1) * 128, :]),
            waits=p4_done, inc="dma_wo"))

    r_xr = [p4_done, p4_done]
    r_ffsb = [p4_done, p4_done]
    r_outsb = [None, None]
    r_att = [st_ffn["sg"], st_ffn["up_ps"]]
    out_stores = []
    for tt in range(8):
        par = tt % 2
        t_x = p.op("Y", lambda e, tt=tt, par=par: e.dma_start(
            out=xr[:, par], in_=x_own[tt * 128:(tt + 1) * 128, :]),
            waits=[r_xr[par]], inc="dma_xr%d" % par)
        t_ff2 = p.op("Y", lambda e, tt=tt, par=par: e.dma_start(
            out=ffsb[:, par], in_=ffn_tok[tt * 128:(tt + 1) * 128, :]),
            waits=[r_ffsb[par]] + (ffn_stores if tt < 2 else []), inc="dma_fb%d" % par)

        def mm_at(e, tt=tt):
            for k in range(16):
                for n4 in range(4):
                    psb = ps_qk if n4 < 2 else ps_vb
                    ins = e.matmul(psb[:, (n4 % 2) * 512:(n4 % 2 + 1) * 512],
                                   ysb[:, k, tt * 128:(tt + 1) * 128],
                                   wo_sb[:, k, n4 * 512:(n4 + 1) * 512],
                                   start=(k == 0), stop=(k == 15))
            return ins
        t_at = p.op("T", mm_at, waits=y_ld + wo_ld + [r_att], inc="T", pe_w=(0, 1, 2, 3))

        cmb = []
        for n4 in range(4):
            psb = ps_qk if n4 < 2 else ps_vb

            def c1(e, n4=n4, par=par, psb=psb):
                return e.tensor_tensor(outsb[:, par, n4 * 512:(n4 + 1) * 512],
                                       psb[:, (n4 % 2) * 512:(n4 % 2 + 1) * 512],
                                       ffsb[:, par, n4 * 512:(n4 + 1) * 512], ALU.add)
            cmb.append(p.op("V", c1, waits=[t_at, t_ff2, r_outsb[par]], inc="V", va_r=(0, 1, 2, 3)))
        t_c2 = p.op("V", lambda e, par=par: e.tensor_tensor(
            outsb[:, par], outsb[:, par], xr[:, par], ALU.add), waits=cmb + [t_x], inc="V")
        r_att = [t_c2] + cmb
        r_xr[par] = t_c2
        r_ffsb[par] = cmb

        t_st = p.op("Y", lambda e, tt=tt, par=par: e.dma_start(
            out=out[tt * 128:(tt + 1) * 128, :], in_=outsb[:, par]),
            waits=[t_c2], inc="dma_st%d" % par)
        out_stores.append(t_st)
        r_outsb[par] = t_st

    pC.close()

    if trunc is not None:
        p.limit = p.marks[trunc] if isinstance(trunc, str) else trunc
    sem_stack = contextlib.ExitStack()
    sems = {}
    for name in sorted(set(list(p.counts.keys()) + ["cc", "dma_fil"])):
        sems[name] = sem_stack.enter_context(nc.semaphore("s_" + name))
    with nc.Block() as block:

        @block.sync
        def _(y):
            p.emit("Y", y, sems)
            if p.limit is not None:
                y.dma_start(out=out, in_=x_own).then_inc(sems["dma_fil"], 16)
                y.wait_ge(sems["dma_fil"], 16)
            else:
                for name in sorted(p.counts):
                    if name.startswith("dma_st"):
                        y.wait_ge(sems[name], p.counts[name])

        @block.tensor
        def _(t):
            p.emit("T", t, sems)

        @block.vector
        def _(v):
            p.emit("V", v, sems)

        @block.scalar
        def _(a):
            p.emit("A", a, sems)

        @block.gpsimd
        def _(g):
            with allow_ici_all_to_all():
                p.emit("P", g, sems)

    sem_stack.close()
    global _LAST_MARKS, _LAST_PROG
    _LAST_MARKS = dict(p.marks)
    _LAST_PROG = p
    root.close()
    return nc


_LAST_MARKS = {}
_LAST_PROG = None
_NC_CACHE = [None]


def make_in_maps(L0_attn_x, L0_attn_Wq, L0_attn_Wk, L0_attn_Wv, L0_attn_Wg, L0_attn_bg,
                 L0_attn_Wo, L0_attn_bo, L0_ffn_w_gate, L0_ffn_w_up, L0_ffn_w_down):
    x = np.ascontiguousarray(np.asarray(L0_attn_x, dtype=np.float32)).reshape(NTOK, DM)
    Wq = np.asarray(L0_attn_Wq, dtype=np.float32)
    Wk = np.asarray(L0_attn_Wk, dtype=np.float32)
    Wv = np.asarray(L0_attn_Wv, dtype=np.float32)
    Wg = np.asarray(L0_attn_Wg, dtype=np.float32)
    bg = np.asarray(L0_attn_bg, dtype=np.float32)
    Wo = np.ascontiguousarray(np.asarray(L0_attn_Wo, dtype=np.float32))
    bo = np.asarray(L0_attn_bo, dtype=np.float32)
    wgate = np.asarray(L0_ffn_w_gate, dtype=np.float32)
    wup = np.asarray(L0_ffn_w_up, dtype=np.float32)
    wdown = np.asarray(L0_ffn_w_down, dtype=np.float32)

    half = 64
    inv_freq = 1.0 / (10000.0 ** (np.arange(half, dtype=np.float64) / half))
    tpos = np.arange(L, dtype=np.float64)
    freqs = tpos[:, None] * inv_freq[None, :]
    cos_t = np.ascontiguousarray(np.cos(freqs).astype(np.float32))
    sin_t = np.sin(freqs).astype(np.float32)
    sin2_t = np.ascontiguousarray(np.concatenate([-sin_t, sin_t], axis=1))
    tri_c = np.ascontiguousarray(np.triu(np.ones((128, 128), np.float32)))
    ident_c = np.ascontiguousarray(np.eye(128, dtype=np.float32))
    ones_c = np.ones((128, 2), np.float32)
    bo_t = np.ascontiguousarray(bo.reshape(16, 128).T)

    wg_t3 = np.ascontiguousarray(
        wgate.reshape(FT, 128, 16, 128).transpose(0, 3, 2, 1).reshape(FT, 128, 16 * 128))
    wu_t3 = np.ascontiguousarray(
        wup.reshape(FT, 128, 16, 128).transpose(0, 3, 2, 1).reshape(FT, 128, 16 * 128))
    wd_t3 = np.ascontiguousarray(
        wdown.reshape(16, 128, FT, 128).transpose(0, 3, 2, 1).reshape(16, 128, FT * 128))

    in_maps = []
    for c in range(NCORES):
        cols = slice(256 * c, 256 * (c + 1))
        wqkv = np.concatenate([Wq[:, cols], Wk[:, cols], Wv[:, cols],
                               Wg[:, 2 * c:2 * c + 2]], axis=1)
        wqkv_t = np.ascontiguousarray(
            wqkv.reshape(16, 128, 770).transpose(1, 0, 2).reshape(128, 16 * 770))
        in_maps.append({
            "x_own": np.ascontiguousarray(x[MYTOK * c:MYTOK * (c + 1)]),
            "wqkv_t": wqkv_t,
            "bg2": np.ascontiguousarray(bg[2 * c:2 * c + 2]),
            "wo": Wo,
            "bo_t": bo_t,
            "wg_t3": wg_t3,
            "wu_t3": wu_t3,
            "wd_t3": wd_t3,
            "cos_t": cos_t,
            "sin2_t": sin2_t,
            "tri_c": tri_c,
            "ident_c": ident_c,
            "ones_c": ones_c,
        })

    return in_maps


def kernel(**inputs):
    in_maps = make_in_maps(**inputs)
    nc = _NC_CACHE[0]
    if nc is None:
        nc = build()
        _NC_CACHE[0] = nc
    with allow_ici_all_to_all():
        res = run_bass_kernel_spmd(nc, in_maps, core_ids=list(range(NCORES)))
    full = np.concatenate([np.asarray(res.results[c]["out"]) for c in range(NCORES)], axis=0)
    return full.reshape(B, L, DM).astype(np.float32)




# version 10
# speedup vs baseline: 1.0483x; 1.0483x over previous
import contextlib
import numpy as np
import ml_dtypes
import concourse.bass as bass
import concourse.mybir as mybir
from concourse.bass_utils import run_bass_kernel_spmd
from concourse.aot_env import allow_ici_all_to_all

F32 = mybir.dt.float32
BF16 = mybir.dt.bfloat16
AF = mybir.ActivationFunctionType
ALU = mybir.AluOpType

NCORES = 8
B = 2
L = 4096
NTOK = B * L
DM = 2048
F = 8192
MYTOK = NTOK // NCORES
TC = 512
NCH = L // TC
FT = F // 128
EPS_LN = 1e-5
EPS_AT = 1e-6
LB_MIN = float(np.log(np.float64(0.8)))
LB_MAX = float(np.log(np.float64(0.9995)))
P2_TILES = 5
NEUTER_YE = False
PIPELINE_P3 = False


def _flat(waits):
    o = []
    for w in waits:
        if w is None:
            continue
        if isinstance(w, list):
            o.extend(x for x in w if x is not None)
        else:
            o.append(w)
    return o


class Prog:
    ENGS = ("T", "V", "A", "P", "Y")

    def __init__(self):
        self.ops = []
        self.counts = {}
        self.last = {}
        self.marks = {}
        self.limit = None
        self.meta = []

    def mark(self, name):
        self.marks[name] = len(self.ops)

    @staticmethod
    def _amt(sem):
        return 16 if sem.startswith("dma") else 1

    def op(self, eng, emit, waits=(), inc=None, pe_w=(), va_r=()):
        waits = tuple(_flat(waits))
        tok = None
        if inc is not None:
            self.counts[inc] = self.counts.get(inc, 0) + self._amt(inc)
            tok = (inc, self.counts[inc])
            self.last[eng] = tok
        self.ops.append((eng, emit, waits, tok))
        self.meta.append((tuple(pe_w), tuple(va_r)))
        return tok

    def snapshot(self):
        return [self.last[e] for e in self.ENGS if e in self.last]

    def emit(self, eng_name, eng_obj, sems):
        hi = {}
        ops = self.ops if self.limit is None else self.ops[:self.limit]
        for eng, emit, waits, tok in ops:
            if eng != eng_name:
                continue
            need = {}
            for (s, v) in waits:
                if need.get(s, -1) < v:
                    need[s] = v
            for s, v in need.items():
                if hi.get(s, -1) < v:
                    eng_obj.wait_ge(sems[s], v)
                    hi[s] = v
            ins = emit(eng_obj)
            if tok is not None:
                ins.then_inc(sems[tok[0]], self._amt(tok[0]))


def build(trunc=None):
    nc = bass.Bass("TRN2", target_bir_lowering=False)

    x_own = nc.dram_tensor("x_own", [MYTOK, DM], F32, kind="ExternalInput").ap()
    wqkv_h = nc.dram_tensor("wqkv_t", [128, 16 * 770], F32, kind="ExternalInput").ap()
    bg2 = nc.dram_tensor("bg2", [2], F32, kind="ExternalInput").ap()
    wo_h = nc.dram_tensor("wo", [DM, DM], F32, kind="ExternalInput").ap()
    bo_h = nc.dram_tensor("bo_t", [128, 16], F32, kind="ExternalInput").ap()
    wg_h = nc.dram_tensor("wg_t3", [FT, 128, 16 * 128], F32, kind="ExternalInput").ap()
    wu_h = nc.dram_tensor("wu_t3", [FT, 128, 16 * 128], F32, kind="ExternalInput").ap()
    wd_h = nc.dram_tensor("wd_t3", [16, 128, FT * 128], F32, kind="ExternalInput").ap()
    cos_h = nc.dram_tensor("cos_t", [L, 64], F32, kind="ExternalInput").ap()
    sin2_h = nc.dram_tensor("sin2_t", [L, 128], F32, kind="ExternalInput").ap()
    tri_h = nc.dram_tensor("tri_c", [128, 128], F32, kind="ExternalInput").ap()
    id_h = nc.dram_tensor("ident_c", [128, 128], F32, kind="ExternalInput").ap()
    ones_h = nc.dram_tensor("ones_c", [128, 2], F32, kind="ExternalInput").ap()

    out = nc.dram_tensor("out", [MYTOK, DM], F32, kind="ExternalOutput").ap()

    xn_loc = nc.dram_tensor("xn_loc", [DM, MYTOK], BF16)
    xn_full_h = nc.dram_tensor("xn_full", [NCORES * DM, MYTOK], BF16, addr_space="Shared")
    xn_full = xn_full_h.ap()
    y_loc_h = nc.dram_tensor("y_loc", [NCORES * 256, MYTOK], BF16)
    y_loc = y_loc_h.ap()
    y_t2_h = nc.dram_tensor("y_t2", [NCORES * 256, MYTOK], BF16)
    y_t2 = y_t2_h.ap()
    ffn_tok = nc.dram_tensor("ffn_tok", [MYTOK, DM], F32)

    p = Prog()
    root = contextlib.ExitStack()

    def sbuf(name, shape, dtype, st=None, side=None):
        kw = {"side": side} if side else {}
        return (st or root).enter_context(nc.sbuf_tensor(name, shape, dtype, **kw))

    ps_qk = root.enter_context(nc.psum_tensor("ps_qk", [128, 1024], F32))
    ps_vb = root.enter_context(nc.psum_tensor("ps_vb", [128, 1024], F32))
    ps_sc = root.enter_context(nc.psum_tensor("ps_sc", [128, 1024], F32))
    ps_ds = root.enter_context(nc.psum_tensor("ps_ds", [128, 512], F32))
    ps_t = root.enter_context(nc.psum_tensor("ps_t", [128, 1024], BF16))

    tri_sb = sbuf("tri_sb", [128, 128], F32)
    id_bf = sbuf("id_bf", [128, 128], BF16)
    ones_bf = sbuf("ones_bf", [128, 2], BF16)
    ones_f32 = sbuf("ones_f32", [128, 1], F32)
    ones_row = sbuf("ones_row", [1, 128], F32)
    bo_sb = sbuf("bo_sb", [128, 16], F32)
    nbg_sb = sbuf("nbg_sb", [128, 2], F32)
    bg_row = sbuf("bg_row", [1, 2], F32)
    gated_a = sbuf("gated_a", [128, P2_TILES, MYTOK], BF16)
    pXN = contextlib.ExitStack()
    xnT_own = sbuf("xnT_own", [128, 16, MYTOK], BF16, pXN, side="right")
    wgu_sb = sbuf("wgu_sb", [128, 2, 2, 16, 128], BF16, pXN, side="right")
    sg_sb = sbuf("sg_sb", [128, 1024], F32, pXN, side="right")
    pA = contextlib.ExitStack()
    wqkv_sb = sbuf("wqkv_sb", [128, 16, 770], BF16, pA)

    t_tri = p.op("Y", lambda e: e.dma_start(out=tri_sb[:], in_=tri_h), inc="dma_tri")
    t_bo = p.op("Y", lambda e: e.dma_start(out=bo_sb[:], in_=bo_h), inc="dma_bo")
    t_idb = p.op("P", lambda e: e.dma_start(out=id_bf[:], in_=id_h), inc="dma_idb")
    t_one = p.op("P", lambda e: e.dma_start(out=ones_bf[:], in_=ones_h), inc="dma_one")
    t_o1 = p.op("V", lambda e: e.memset(ones_f32[:], 1.0), inc="V")
    t_or = p.op("V", lambda e: e.memset(ones_row[:], 1.0), inc="V")
    t_bg = p.op("Y", lambda e: e.dma_start(out=bg_row[:], in_=bg2.unsqueeze(0)),
                inc="dma_bg")
    t_pb = p.op("T", lambda e: e.matmul(ps_ds[:, 0:2], ones_row[:], bg_row[:],
                                        start=True, stop=True), waits=[t_bg, t_or], inc="T", pe_w=(6,))
    t_nbg = p.op("V", lambda e: e.tensor_scalar(nbg_sb[:], ps_ds[:, 0:2], -1.0, None, ALU.mult),
                 waits=[t_pb], inc="V", va_r=(6,))
    t_wqkv = p.op("P", lambda e: e.dma_start(
        out=wqkv_sb[:].rearrange("p a b -> p (a b)"), in_=wqkv_h), inc="dma_qkv")

    pP1 = contextlib.ExitStack()
    xf = sbuf("xf", [128, 2, DM], F32, pP1)
    st6 = sbuf("st6", [128, 24], F32, pP1)
    agg = sbuf("agg", [128, 2], F32, pP1)
    rstd = sbuf("rstd", [128, 1], F32, pP1)
    nmr = sbuf("nmr", [128, 1], F32, pP1)
    xn_tok = sbuf("xn_tok", [128, 2, DM], BF16, pP1)

    r_xf = [None, None]
    r_tpchain = None
    prev_apply = None
    for t in range(8):
        pb = t % 2
        tok_ld = p.op("Y", lambda e, t=t, pb=pb: e.dma_start(
            out=xf[:, pb], in_=x_own[t * 128:(t + 1) * 128, :]), waits=[r_xf[pb]],
            inc="dma_x%d" % pb)

        def stats(e, pb=pb):
            for i in range(4):
                ins = e.bn_stats(st6[:, i * 6:(i + 1) * 6], xf[:, pb, i * 512:(i + 1) * 512])
            return ins
        tok_st = p.op("V", stats, waits=[tok_ld, prev_apply], inc="V")
        tok_ag = p.op("V", lambda e: e.bn_aggr(agg[:], st6[:]), waits=[tok_st], inc="V")
        tok_r0 = p.op("V", lambda e: e.tensor_scalar(rstd[:], agg[:, 1:2], EPS_LN, None, ALU.add),
                      waits=[tok_ag], inc="V")
        tok_r1 = p.op("A", lambda e: e.activation(rstd[:], rstd[:], AF.Ln), waits=[tok_r0], inc="A")
        tok_r2 = p.op("A", lambda e: e.activation(rstd[:], rstd[:], AF.Exp, scale=-0.5),
                      waits=[tok_r1], inc="A")
        tok_nm = p.op("V", lambda e: e.tensor_scalar(nmr[:], agg[:, 0:1], rstd[:], -1.0,
                                                     ALU.mult, ALU.mult), waits=[tok_r2], inc="V")
        tok_ap = p.op("A", lambda e, pb=pb: e.activation(
            xn_tok[:, pb], xf[:, pb], AF.Identity, bias=nmr[:], scale=rstd[:]),
            waits=[tok_nm], inc="A")
        prev_apply = tok_ap
        r_xf[pb] = tok_ap

        for g in range(2):
            def tr(e, pb=pb, g=g):
                for kk in range(8):
                    ins = e.transpose(ps_t[:, kk * 128:(kk + 1) * 128],
                                      xn_tok[:, pb, (g * 8 + kk) * 128:(g * 8 + kk + 1) * 128],
                                      id_bf[:])
                return ins
            tok_tr = p.op("T", tr, waits=[tok_ap, r_tpchain, t_idb], inc="T", pe_w=(7,))

            def ev(e, t=t, g=g):
                return e.tensor_copy(
                    xnT_own[:, g * 8:(g + 1) * 8, t * 128:(t + 1) * 128],
                    ps_t[:].rearrange("p (a b) -> p a b", a=8, b=128))
            r_tpchain = p.op("V", ev, waits=[tok_tr], inc="V", va_r=(7,))

    xn_done = [r_tpchain]
    ag_in = []
    for k in range(16):
        ag_in.append(p.op("Y", lambda e, k=k: e.dma_start(
            out=xn_loc[k * 128:(k + 1) * 128, :], in_=xnT_own[:, k, :]),
            waits=xn_done, inc="dma_xnl"))
    p.op("P", lambda e: e.collective_compute(
        "AllGather", ALU.bypass, replica_groups=[list(range(NCORES))],
        ins=[xn_loc.ap().opt()], outs=[xn_full_h.ap().opt()]), waits=ag_in, inc="cc")

    st_ffn = {"wgu": [None, None], "sg": None, "up_ps": None}

    def ffn_a_tile(ff, gated_dst, extra_waits=()):
        dbuf = ff % 2
        w1 = p.op("P", lambda e, ff=ff, dbuf=dbuf: e.dma_start(
            out=wgu_sb[:, 0, dbuf].rearrange("p a b -> p (a b)"), in_=wg_h[ff]),
            waits=[st_ffn["wgu"][dbuf]] + list(extra_waits), inc="dma_wg%d" % dbuf)
        w2 = p.op("P", lambda e, ff=ff, dbuf=dbuf: e.dma_start(
            out=wgu_sb[:, 1, dbuf].rearrange("p a b -> p (a b)"), in_=wu_h[ff]),
            waits=[st_ffn["wgu"][dbuf]], inc="dma_wu%d" % dbuf)

        def mm_gate(e, dbuf=dbuf):
            for k in range(16):
                for h2 in range(2):
                    ins = e.matmul(ps_qk[:, h2 * 512:(h2 + 1) * 512], wgu_sb[:, 0, dbuf, k],
                                   xnT_own[:, k, h2 * 512:(h2 + 1) * 512],
                                   start=(k == 0), stop=(k == 15))
            return ins
        tg = p.op("T", mm_gate,
                  waits=[w1, st_ffn["sg"]] + xn_done + list(extra_waits), inc="T", pe_w=(0, 1))

        def mm_up(e, dbuf=dbuf):
            for k in range(16):
                for h2 in range(2):
                    ins = e.matmul(ps_vb[:, h2 * 512:(h2 + 1) * 512], wgu_sb[:, 1, dbuf, k],
                                   xnT_own[:, k, h2 * 512:(h2 + 1) * 512],
                                   start=(k == 0), stop=(k == 15))
            return ins
        tu = p.op("T", mm_up, waits=[w2, st_ffn["up_ps"]] + list(extra_waits), inc="T", pe_w=(2, 3))

        tsg = p.op("A", lambda e: e.activation(sg_sb[:], ps_qk[:], AF.Silu),
                   waits=[tg, st_ffn["up_ps"]], inc="A", va_r=(0, 1))
        st_ffn["sg"] = tsg

        tgm = p.op("V", lambda e, gated_dst=gated_dst: e.tensor_tensor(
            gated_dst, sg_sb[:], ps_vb[:], ALU.mult), waits=[tsg, tu], inc="V", va_r=(2, 3))
        st_ffn["up_ps"] = tgm
        st_ffn["wgu"][dbuf] = tgm
        return tgm

    gated_toks = []
    for ff in range(P2_TILES):
        gated_toks.append(ffn_a_tile(ff, gated_a[:, ff, :]))

    pP1.close()
    xnT_st = sbuf("xnT_st", [128, 2, 2, 16, TC], BF16, pA)
    ropeC = sbuf("ropeC", [128, 4, 64], F32, pA)
    ropeS = sbuf("ropeS", [128, 4, 128], F32, pA)
    qkf = sbuf("qkf", [128, 8, 512], F32, pA)
    phis = sbuf("phis", [128, 2, 512], F32, pA)
    qkt = sbuf("qkt", [128, 8, 512], BF16, pA)
    qkfT = sbuf("qkfT", [128, 8, 512], BF16, pA)
    vsb = sbuf("vsb", [128, 8, 2, 132], BF16, pA)
    aw_sb = sbuf("aw_sb", [128, 2, 128], BF16, pA)
    ssb = sbuf("ssb", [128, 4, 132], F32, pA)
    ssb_bf = sbuf("ssb_bf", [128, 4, 132], BF16, pA)
    y_tokb = sbuf("y_tokb", [128, 16, 128], BF16, pA)
    yT_sb = sbuf("yT_sb", [128, 2, 2, 512], BF16, pA)
    bsb = sbuf("bsb", [128, 2, 8], F32, pA)
    ebs = sbuf("ebs", [128, 2, 8], F32, pA)
    lb_sb = sbuf("lb_sb", [128, 2, 8], F32, pA)
    psc_sb = sbuf("psc_sb", [128, 4, 8], F32, pA)
    plr_sb = sbuf("plr_sb", [1, 16], F32, pA)
    plast_sb = sbuf("plast_sb", [128, 2, 8], F32, pA)
    den_sb = sbuf("den_sb", [128, 2, 1], F32, pA)

    t_vones = p.op("V", lambda e: e.memset(
        vsb[:].rearrange("p a b c -> p (a b c)"), 1.0), inc="V")

    S = dict(
        r_xnT=[None, None], r_rope=None,
        r_qk=[st_ffn["sg"], st_ffn["sg"]], r_vb=[st_ffn["up_ps"], st_ffn["up_ps"]],
        r_sc=[[], []], r_dsl=[[t_vones], [t_vones]], r_clog=[t_nbg], r_psc=[],
        r_tpchain=xn_done[0], r_qkf=None, r_phis=None,
        r_qkt=None, r_qkfT=None, r_vsb=t_vones,
        r_aw=[None, None], r_ytok=None, r_yT=None,
        r_plr=None, r_plbc=t_nbg, r_den=[None, None], r_bsb1=None,
        state_tok=[None, None, None, None], state_bf_tok=[None, None, None, None],
        g_cnt=0,
    )
    y_stores = []

    def p3_proj(ci, gi):
        ctx = {"sl": gi % 2, "ci": ci, "gi": gi, "psum_reads": []}
        sl = gi % 2
        l0 = ci * TC
        w_first = [("cc", 1)] if gi == 0 else []
        xn_toks = []
        for b in range(2):
            t0 = b * L + ci * TC
            rank, col0 = divmod(t0, MYTOK)
            for kq in range(2):
                xn_toks.append(p.op("Y", lambda e, rank=rank, col0=col0, sl=sl, b=b, kq=kq:
                    e.dma_start(
                        out=xnT_st[:, sl, b, kq * 8:(kq + 1) * 8],
                        in_=xn_full[rank * DM + kq * 1024: rank * DM + (kq + 1) * 1024,
                                    col0:col0 + TC].rearrange("(k p) n -> p k n", p=128)),
                    waits=[S["r_xnT"][sl]] + w_first, inc="dma_xn%d" % sl))
        tok_rc = p.op("Y", lambda e, l0=l0: e.dma_start(
            out=ropeC[:],
            in_=cos_h[l0:l0 + TC, :].rearrange("(s p) f -> p s f", p=128)),
            waits=[S["r_rope"]], inc="dma_rc0")
        tok_rs2 = p.op("Y", lambda e, l0=l0: e.dma_start(
            out=ropeS[:],
            in_=sin2_h[l0:l0 + TC, :].rearrange("(s p) f -> p s f", p=128)),
            waits=[S["r_rope"]], inc="dma_rs0")

        ctx.update(xn_toks=xn_toks, tok_rc=tok_rc, tok_rs2=tok_rs2)
        vb_toks = []
        for s8 in range(8):
            b, s = divmod(s8, 4)
            pb = s8 % 2

            def mm_vb(e, b=b, s=s, sl=sl, pb=pb):
                for k in range(16):
                    ins = e.matmul(ps_vb[:, pb * 512:pb * 512 + 258],
                                   xnT_st[:, sl, b, k, s * 128:(s + 1) * 128],
                                   wqkv_sb[:, k, 512:770], start=(k == 0), stop=(k == 15))
                return ins
            t_mm = p.op("T", mm_vb, waits=xn_toks + [t_wqkv, S["r_vb"][pb], S["r_plbc"]],
                        inc="T", pe_w=(2 + pb,))

            t_evv = p.op("A", lambda e, s8=s8, sl=sl, pb=pb: e.copy(
                vsb[:, s8, :, 0:128],
                ps_vb[:, pb * 512:pb * 512 + 256].rearrange("p (h d) -> p h d", h=2, d=128)),
                waits=[t_mm, S["r_vsb"]], inc="A", va_r=(2 + pb,))
            t_cb = p.op("V", lambda e, s8=s8, sl=sl, pb=pb: e.tensor_copy(
                bsb[:, :, s8], ps_vb[:, pb * 512 + 256:pb * 512 + 258]),
                waits=[t_mm, S["r_bsb1"]], inc="V", va_r=(2 + pb,))
            S["r_vb"][pb] = [t_evv, t_cb]
            vb_toks.append([t_evv, t_cb])
            ctx["psum_reads"].extend([t_evv, t_cb])
        ctx["vb_toks"] = vb_toks

        return ctx

    def p3_mid(ctx):
        sl = ctx["sl"]
        w_exp = _flat(ctx["vb_toks"]) + [t_nbg]
        eb_toks = []
        for h in range(2):
            eb_toks.append(p.op("A", lambda e, h=h, sl=sl: e.activation(
                ebs[:, h, :], bsb[:, h, :], AF.Exp, scale=-1.0, bias=nbg_sb[:, h:h + 1]),
                waits=w_exp + [S["r_ebs"]] if "r_ebs" in S else w_exp, inc="A"))
        S["r_bsb1"] = eb_toks
        t_t2 = p.op("V", lambda e: e.tensor_scalar(
            ebs[:].rearrange("p a b -> p (a b)"), ebs[:].rearrange("p a b -> p (a b)"),
            1.0, None, ALU.add), waits=eb_toks, inc="V")
        t_ln = p.op("A", lambda e: e.activation(
            lb_sb[:].rearrange("p a b -> p (a b)"), ebs[:].rearrange("p a b -> p (a b)"),
            AF.Ln), waits=[t_t2] + S.get("r_lb", []), inc="A")
        S["r_ebs"] = t_ln
        t_lb1 = p.op("V", lambda e: e.tensor_scalar(
            lb_sb[:].rearrange("p a b -> p (a b)"), lb_sb[:].rearrange("p a b -> p (a b)"),
            -1.0, LB_MAX, ALU.mult, ALU.min), waits=[t_ln], inc="V")
        t_lb = p.op("V", lambda e: e.tensor_scalar(
            lb_sb[:].rearrange("p a b -> p (a b)"), lb_sb[:].rearrange("p a b -> p (a b)"),
            LB_MIN, None, ALU.max), waits=[t_lb1], inc="V")

        t_clog = p.op("T", lambda e: e.matmul(
            ps_ds[:, 264:280], tri_sb[:], lb_sb[:].rearrange("p a b -> p (a b)"),
            start=True, stop=True),
            waits=[t_lb, t_tri] + S["r_clog"] + S["r_dsl"][0] + S["r_dsl"][1], inc="T",
            pe_w=(6,))
        t_sum = p.op("T", lambda e: e.matmul(
            ps_ds[0:1, 280:296], ones_f32[:], lb_sb[:].rearrange("p a b -> p (a b)"),
            start=True, stop=True), waits=[t_lb, t_o1], inc="T", pe_w=(6,))
        S["r_lb"] = [t_clog, t_sum]
        t_p1 = p.op("A", lambda e: e.activation(
            psc_sb[:, 0:2, :].rearrange("p a b -> p (a b)"), ps_ds[:, 264:280], AF.Exp),
            waits=[t_clog, t_sum] + S["r_psc"], inc="A", va_r=(6,))
        t_p2 = p.op("A", lambda e: e.activation(
            psc_sb[:, 2:4, :].rearrange("p a b -> p (a b)"), ps_ds[:, 264:280], AF.Exp,
            scale=-1.0), waits=[t_clog, t_sum] + S["r_psc"], inc="A", va_r=(6,))
        t_pe = p.op("V", lambda e: e.tensor_copy(plr_sb[:], ps_ds[0:1, 280:296]),
                    waits=[t_sum, S["r_plr"]], inc="V", va_r=(6,))
        t_pbc = p.op("T", lambda e: e.matmul(
            ps_ds[:, 296:312], ones_row[:], plr_sb[:], start=True, stop=True),
            waits=[t_pe, t_p1, t_p2, S["r_plbc"]], inc="T", pe_w=(6,))
        S["r_plr"] = t_pbc
        t_pl = p.op("A", lambda e: e.activation(
            plast_sb[:].rearrange("p a b -> p (a b)"), ps_ds[:, 296:312], AF.Exp),
            waits=[t_pbc] + S["state_tok"], inc="A", va_r=(6,))
        S["r_plbc"] = t_pl
        S["r_clog"] = [t_pl, t_pe]
        ctx.update(t_p1=t_p1, t_p2=t_p2, t_pe=t_pe, t_pl=t_pl)
        return ctx

    def p3_qk(ctx):
        sl = ctx["sl"]
        xn_toks = ctx["xn_toks"]
        tok_rc = ctx["tok_rc"]
        tok_rs2 = ctx["tok_rs2"]
        ctx["f5s"] = {}
        for s8 in range(8):
            b, s = divmod(s8, 4)
            pb = s8 % 2

            def mm_qk(e, b=b, s=s, sl=sl, pb=pb):
                for k in range(16):
                    ins = e.matmul(ps_qk[:, pb * 512:(pb + 1) * 512],
                                   xnT_st[:, sl, b, k, s * 128:(s + 1) * 128],
                                   wqkv_sb[:, k, 0:512], start=(k == 0), stop=(k == 15))
                return ins
            t_mmqk = p.op("T", mm_qk, waits=xn_toks + [t_wqkv, S["r_qk"][pb]], inc="T",
                          pe_w=(pb,))

            def rope1(e, s=s, sl=sl, pb=pb):
                src2 = ps_qk[:, pb * 512:(pb + 1) * 512].rearrange(
                    "p (g two f) -> p g two f", g=4, two=2, f=64)
                sw = src2[:, :, ::-1, :]
                sinb = ropeS[:, s].rearrange("p (two f) -> p two f", two=2) \
                    .unsqueeze(1).broadcast_to((128, 4, 2, 64))
                return e.tensor_tensor(
                    phis[:, 0].rearrange("p (g two f) -> p g two f", g=4, two=2, f=64),
                    sw, sinb, ALU.mult)
            t_r1 = p.op("V", rope1, waits=[t_mmqk, tok_rs2, S["r_phis"]], inc="V", va_r=(pb,))

            def rope2(e, s=s, sl=sl, pb=pb):
                src2 = ps_qk[:, pb * 512:(pb + 1) * 512].rearrange(
                    "p (g two f) -> p g two f", g=4, two=2, f=64)
                cosb = ropeC[:, s].unsqueeze(1).unsqueeze(2).broadcast_to((128, 4, 2, 64))
                return e.tensor_tensor(
                    phis[:, 1].rearrange("p (g two f) -> p g two f", g=4, two=2, f=64),
                    src2, cosb, ALU.mult)
            t_r2 = p.op("V", rope2, waits=[t_mmqk, tok_rc, S["r_phis"]], inc="V", va_r=(pb,))
            S["r_qk"][pb] = [t_r1, t_r2]
            ctx["psum_reads"].extend([t_r1, t_r2])
            S["r_rope"] = t_r2

            t_r3 = p.op("V", lambda e, s8=s8: e.tensor_tensor(
                qkf[:, s8], phis[:, 0], phis[:, 1], ALU.add),
                waits=[t_r1, t_r2, S["r_qkf"]], inc="V")
            S["r_phis"] = t_r3
            S["r_xnT"][sl] = t_mmqk

            t_f1 = p.op("V", lambda e, s8=s8: e.tensor_scalar(
                phis[:, 0], qkf[:, s8], 0.0, None, ALU.min),
                waits=[t_r3, S["r_phis"]], inc="V")
            t_f2 = p.op("A", lambda e: e.activation(phis[:, 1], phis[:, 0], AF.Exp),
                        waits=[t_f1], inc="A")

            def f3(e, s8=s8):
                pv = psc_sb[:, :, s8].unsqueeze(2).broadcast_to((128, 4, 128))
                return e.scalar_tensor_tensor(
                    phis[:, 0].rearrange("p (g f) -> p g f", g=4, f=128),
                    qkf[:, s8].rearrange("p (g f) -> p g f", g=4, f=128),
                    1.0, pv, ALU.add, ALU.mult)
            t_f3 = p.op("V", f3, waits=[t_f1, t_f2, ctx["t_p1"], ctx["t_p2"]], inc="V")
            S["r_qkf"] = t_f3

            def f4(e, s8=s8):
                pv = psc_sb[:, :, s8].unsqueeze(2).broadcast_to((128, 4, 128))
                return e.tensor_tensor(
                    phis[:, 1].rearrange("p (g f) -> p g f", g=4, f=128),
                    phis[:, 1].rearrange("p (g f) -> p g f", g=4, f=128),
                    pv, ALU.mult)
            t_f4 = p.op("V", f4, waits=[t_f2, ctx["t_p1"], ctx["t_p2"]], inc="V")

            t_f5 = p.op("V", lambda e, s8=s8: e.tensor_tensor(
                qkt[:, s8], phis[:, 0], phis[:, 1], ALU.max),
                waits=[t_f3, t_f4, S["r_qkt"]], inc="V")
            S["r_phis"] = t_f5
            ctx["f5s"][s8] = t_f5
            ctx["t_f34l"] = [t_f3, t_f4]
        return ctx

    def p3_scan(ctx, guard=()):
        sl = ctx["sl"]
        ci = ctx["ci"]
        vb_toks = ctx["vb_toks"]
        t_p1, t_p2, t_pe, t_pl = ctx["t_p1"], ctx["t_p2"], ctx["t_pe"], ctx["t_pl"]
        if ci == 0:
            t_rs = p.op("V", lambda e: e.memset(ssb[:].rearrange("p a b -> p (a b)"), 0.0),
                        waits=S["state_tok"] + S["state_bf_tok"], inc="V")
            t_rsb = p.op("V", lambda e: e.memset(ssb_bf[:].rearrange("p a b -> p (a b)"), 0.0),
                         waits=S["state_tok"] + S["state_bf_tok"], inc="V")
            S["state_tok"] = [t_rs] * 4
            S["state_bf_tok"] = [t_rsb] * 4

        ye_toks = {0: [], 1: []}
        t_ds_last = None
        order = [0, 4, 1, 5, 2, 6, 3, 7]
        for oi, s8 in enumerate(order):
            b, s = divmod(s8, 4)
            t_f5 = ctx["f5s"][s8]

            def trq(e, sl=sl, s8=s8):
                for g in range(4):
                    ins = e.transpose(ps_t[:, g * 128:(g + 1) * 128],
                                      qkt[:, s8, g * 128:(g + 1) * 128], id_bf[:])
                return ins
            t_tr = p.op("T", trq, waits=[t_f5, S["r_tpchain"], t_idb] +
                        list(guard if oi == 0 else ()), inc="T", pe_w=(7,))
            t_evq = p.op("V", lambda e, sl=sl, s8=s8: e.tensor_copy(
                qkfT[:, s8], ps_t[:, 0:512]),
                waits=[t_tr, S["r_qkfT"]], inc="V", va_r=(7,))
            S["r_tpchain"] = t_evq

            for h in range(2):
                chain = b * 2 + h
                gpar = S["g_cnt"] % 2
                bank = ps_sc[:, gpar * 512:(gpar + 1) * 512]
                dslot = S["g_cnt"] % 2
                dsl_ap = ps_ds[:, dslot * 132:dslot * 132 + 129]
                qf = qkfT[:, s8, h * 128:(h + 1) * 128]
                kf = qkfT[:, s8, 256 + h * 128:256 + h * 128 + 128]
                kt = qkt[:, s8, 256 + h * 128:256 + h * 128 + 128]
                vv = vsb[:, s8, h]

                t_A = p.op("T", lambda e, bank=bank, kf=kf, qf=qf: e.matmul(
                    bank[:, 0:128], kf, qf, start=True, stop=True),
                    waits=[t_evq] + S["r_sc"][gpar], inc="T", pe_w=(4 + gpar,))
                t_aw = p.op("V", lambda e, bank=bank, gpar=gpar: e.tensor_tensor(
                    aw_sb[:, gpar], bank[:, 0:128], tri_sb[:], ALU.mult),
                    waits=[t_A, S["r_aw"][gpar], t_tri], inc="V", va_r=(4 + gpar,))
                t_y1 = p.op("T", lambda e, bank=bank, gpar=gpar, vv=vv: e.matmul(
                    bank[:, 128:256], aw_sb[:, gpar], vv[:, 0:128], start=True, stop=False),
                    waits=[t_aw] + vb_toks[s8], inc="T", pe_w=(4 + gpar,))
                t_y2 = p.op("T", lambda e, bank=bank, qf=qf, chain=chain: e.matmul(
                    bank[:, 128:256], qf, ssb_bf[:, chain, 0:128], start=False, stop=True),
                    waits=[S["state_bf_tok"][chain]], inc="T", pe_w=(4 + gpar,))
                t_d1 = p.op("T", lambda e, bank=bank, gpar=gpar: e.matmul(
                    bank[:, 256:257], aw_sb[:, gpar], ones_bf[:, 0:1],
                    start=True, stop=False), waits=[t_aw, t_one], inc="T", pe_w=(4 + gpar,))
                t_d2 = p.op("T", lambda e, bank=bank, qf=qf, chain=chain: e.matmul(
                    bank[:, 256:257], qf, ssb_bf[:, chain, 128:129], start=False, stop=True),
                    waits=[S["state_bf_tok"][chain]], inc="T", pe_w=(4 + gpar,))
                t_ds = p.op("T", lambda e, dsl_ap=dsl_ap, kt=kt, vv=vv: e.matmul(
                    dsl_ap, kt, vv[:, 0:129], start=True, stop=True),
                    waits=[t_f5, t_p1, t_p2, t_pe, t_pl] + S["r_dsl"][0] + S["r_dsl"][1]
                    + vb_toks[s8], inc="T", pe_w=(6,))
                t_ds_last = t_ds

                t_de = p.op("V", lambda e, bank=bank, gpar=gpar: e.tensor_scalar(
                    den_sb[:, gpar], bank[:, 256:257], EPS_AT, None, ALU.add),
                    waits=[t_d2, S["r_den"][gpar]], inc="V", va_r=(4 + gpar,))
                t_dr = p.op("V", lambda e, gpar=gpar: e.reciprocal(
                    den_sb[:, gpar], den_sb[:, gpar]), waits=[t_de], inc="V")
                yslot = s8 * 2 + h
                t_ye = p.op("A", lambda e, bank=bank, gpar=gpar, yslot=yslot: e.activation(
                    y_tokb[:, yslot], bank[:, 128:256], AF.Identity, scale=den_sb[:, gpar]),
                    waits=[t_y2, t_dr, S["r_ytok"]], inc="A", va_r=(4 + gpar,))
                S["r_den"][gpar] = t_ye
                S["r_sc"][gpar] = [t_aw, t_ye, t_dr]
                S["r_aw"][gpar] = [t_y1, t_d1]
                ye_toks[b].append(t_ye)

                t_s1 = p.op("V", lambda e, chain=chain, dsl_ap=dsl_ap: e.tensor_tensor(
                    ssb[:, chain, 0:129], ssb[:, chain, 0:129], dsl_ap, ALU.add),
                    waits=[t_ds, S["state_tok"][chain]], inc="V", va_r=(6,))
                t_s2 = p.op("V", lambda e, chain=chain, h=h, s8=s8: e.tensor_scalar(
                    ssb[:, chain, 0:129], ssb[:, chain, 0:129], plast_sb[:, h, s8:s8 + 1],
                    None, ALU.mult), waits=[t_s1, t_pl], inc="V")
                S["state_tok"][chain] = t_s2
                S["r_dsl"][dslot] = [t_s1]
                t_sb2 = p.op("A", lambda e, chain=chain: e.copy(
                    ssb_bf[:, chain, 0:129], ssb[:, chain, 0:129]),
                    waits=[t_s2, t_y2, t_d2], inc="A")
                S["state_bf_tok"][chain] = t_sb2
                S["g_cnt"] += 1

        S["r_vsb"] = S["state_bf_tok"][3]
        S["r_qkt"] = S["state_bf_tok"][3]
        S["r_qkfT"] = S["state_bf_tok"][3]
        S["r_psc"] = ctx["t_f34l"] + [t_ds_last]

        for b in range(2):
            def tr_y(e, b=b):
                for j in range(8):
                    ins = e.transpose(ps_t[:, j * 128:(j + 1) * 128],
                                      y_tokb[:, b * 8 + j], id_bf[:])
                return ins
            t_yT = p.op("T", tr_y, waits=ye_toks[b] + [S["r_tpchain"]], inc="T", pe_w=(7,))

            def ev_y(e, b=b):
                src_v = ps_t[:].rearrange("p (s2 h d) -> p s2 h d", s2=4, h=2, d=128) \
                    .rearrange("p s2 h d -> p h s2 d")
                return e.tensor_copy(
                    yT_sb[:, b].rearrange("p h (s2 d) -> p h s2 d", s2=4, d=128), src_v)
            t_yTe = p.op("V", ev_y, waits=[t_yT, S["r_yT"]], inc="V", va_r=(7,))
            S["r_tpchain"] = t_yTe
            S["r_ytok"] = t_yT

            t0b = b * L + ci * TC
            rank, col0 = divmod(t0b, MYTOK)
            for h in range(2):
                y_stores.append(p.op("Y", lambda e, rank=rank, col0=col0, b=b, h=h:
                    e.dma_start(
                        out=y_loc[rank * 256 + h * 128: rank * 256 + (h + 1) * 128,
                                  col0:col0 + TC],
                        in_=yT_sb[:, b, h, :]), waits=[t_yTe], inc="dma_ys0"))
            S["r_yT"] = y_stores[-1]

    p.mark("preP3")
    for gi, ci in enumerate(range(NCH)):
        ctx = p3_proj(ci, gi)
        p3_mid(ctx)
        p3_qk(ctx)
        p3_scan(ctx)
        p.mark("chunk%d" % gi)
    p.mark("P3done")

    r_qk = S["r_qk"]
    r_vb = S["r_vb"]
    r_sc = S["r_sc"]
    r_dsl = S["r_dsl"]
    r_clog = S["r_clog"]

    p3_done = p.snapshot()
    p.op("P", lambda e: e.collective_compute(
        "AllToAll", ALU.bypass, replica_groups=[list(range(NCORES))],
        ins=[y_loc_h.ap().opt()], outs=[y_t2_h.ap().opt()]),
        waits=y_stores + [("cc", 1)], inc="cc")

    p.mark("A2A")
    p3_ps_free = _flat([r_qk[0], r_qk[1], r_vb[0], r_vb[1]])
    pA.close()

    pB = contextlib.ExitStack()
    gated_b = sbuf("gated_b", [128, FT - P2_TILES, MYTOK], BF16, pB)
    for ff in range(P2_TILES, FT):
        extra = (p3_ps_free + p3_done) if ff == P2_TILES else ()
        gated_toks.append(ffn_a_tile(ff, gated_b[:, ff - P2_TILES, :], extra))
    gated_all = gated_toks[-1]

    def gated_get(f):
        if f < P2_TILES:
            return gated_a[:, f, :]
        return gated_b[:, f - P2_TILES, :]

    p.mark("P4a")
    pXN.close()
    pWO = contextlib.ExitStack()
    wo_a = sbuf("wo_a", [128, 8, DM], BF16, pWO, side="right")
    id_f32 = sbuf("id_f32", [128, 128], F32, pB)
    t_idf = p.op("Y", lambda e: e.dma_start(out=id_f32[:], in_=id_h),
                 waits=p3_done, inc="dma_idf")
    wd_sb = sbuf("wd_sb", [128, 2, FT, 128], BF16, pB)
    ffn_f = sbuf("ffn_f", [128, 2, 512], F32, pB)
    fftT = sbuf("fftT", [128, 2, 512], F32, pB)

    r_wd = [[gated_all] + p3_done, [gated_all] + p3_done]
    r_dn = [r_sc[0], r_sc[1]]
    r_ff = [None, None]
    r_fftT = [None, None]
    r_dstr = [r_dsl[0] + r_dsl[1] + r_clog, None]
    last_dn = None
    ffn_stores = []
    dn_cnt = 0
    for dout in range(16):
        dbuf = dout % 2
        t_wd = p.op("P", lambda e, dout=dout, dbuf=dbuf: e.dma_start(
            out=wd_sb[:, dbuf].rearrange("p a b -> p (a b)"), in_=wd_h[dout]),
            waits=[r_wd[dbuf]], inc="dma_wd%d" % dbuf)
        for half in range(2):
            par = dn_cnt % 2
            bank = ps_sc[:, par * 512:(par + 1) * 512]

            def mm_dn(e, dbuf=dbuf, half=half, bank=bank):
                for f in range(FT):
                    ins = e.matmul(bank[:, 0:512], wd_sb[:, dbuf, f],
                                   gated_get(f)[:, half * 512:(half + 1) * 512],
                                   start=(f == 0), stop=(f == FT - 1))
                return ins
            t_dn = p.op("T", mm_dn, waits=[t_wd, gated_all, r_dn[par]], inc="T", pe_w=(4 + par,))
            last_dn = t_dn

            t_fe = p.op("A", lambda e, par=par, bank=bank, dout=dout: e.activation(
                ffn_f[:, par], bank[:, 0:512], AF.Identity, bias=bo_sb[:, dout:dout + 1]),
                waits=[t_dn, r_ff[par], t_bo], inc="A", va_r=(4 + par,))
            r_dn[par] = t_fe

            def trf(e, par=par):
                for i in range(4):
                    ins = e.transpose(ps_ds[:, i * 128:(i + 1) * 128],
                                      ffn_f[:, par, i * 128:(i + 1) * 128], id_f32[:])
                return ins
            t_tf = p.op("T", trf, waits=[t_fe, r_dstr[0], r_dstr[1], t_idf], inc="T", pe_w=(6,))
            r_ff[par] = t_tf

            t_te = p.op("V", lambda e, par=par: e.tensor_copy(fftT[:, par], ps_ds[:, 0:512]),
                        waits=[t_tf, r_fftT[par]], inc="V", va_r=(6,))
            r_dstr = [t_te, None]

            t_sf = p.op("Y", lambda e, par=par, half=half, dout=dout: e.dma_start(
                out=ffn_tok[half * 512:(half + 1) * 512, dout * 128:(dout + 1) * 128]
                .rearrange("(i p) c -> p i c", p=128),
                in_=fftT[:, par].rearrange("p (i c) -> p i c", i=4, c=128)),
                waits=[t_te], inc="dma_ff%d" % par)
            ffn_stores.append(t_sf)
            r_fftT[par] = t_sf
            dn_cnt += 1
        r_wd[dbuf] = last_dn

    p.mark("P4b")
    wo_a_ld = []
    for k in range(8):
        wo_a_ld.append(p.op("P", lambda e, k=k: e.dma_start(
            out=wo_a[:, k, :], in_=wo_h[k * 128:(k + 1) * 128, :]),
            waits=[gated_all] + p3_done, inc="dma_wo"))

    p4_done = p.snapshot()
    pB.close()
    pC = contextlib.ExitStack()
    ysb = sbuf("ysb", [128, 16, MYTOK], BF16, pC)
    wo_sb = sbuf("wo_sb", [128, 8, DM], BF16, pC)
    xr = sbuf("xr", [128, 2, DM], F32, pC)
    ffsb = sbuf("ffsb", [128, 2, DM], F32, pC)
    outsb = sbuf("outsb", [128, 2, DM], F32, pC)

    y_ld = []
    for k in range(16):
        y_ld.append(p.op("Y", lambda e, k=k: e.dma_start(
            out=ysb[:, k, :], in_=y_t2[k * 128:(k + 1) * 128, :]),
            waits=[("cc", 2)] + p4_done, inc="dma_y2"))
    wo_ld = list(wo_a_ld)
    for k in range(8, 16):
        wo_ld.append(p.op("P", lambda e, k=k: e.dma_start(
            out=wo_sb[:, k - 8, :], in_=wo_h[k * 128:(k + 1) * 128, :]),
            waits=p4_done, inc="dma_wo"))

    r_xr = [p4_done, p4_done]
    r_ffsb = [p4_done, p4_done]
    r_outsb = [None, None]
    r_att = [st_ffn["sg"], st_ffn["up_ps"]]
    out_stores = []
    for tt in range(8):
        par = tt % 2
        t_x = p.op("Y", lambda e, tt=tt, par=par: e.dma_start(
            out=xr[:, par], in_=x_own[tt * 128:(tt + 1) * 128, :]),
            waits=[r_xr[par]], inc="dma_xr%d" % par)
        t_ff2 = p.op("Y", lambda e, tt=tt, par=par: e.dma_start(
            out=ffsb[:, par], in_=ffn_tok[tt * 128:(tt + 1) * 128, :]),
            waits=[r_ffsb[par]] + (ffn_stores if tt < 2 else []), inc="dma_fb%d" % par)

        def mm_at(e, tt=tt):
            for k in range(16):
                wsrc = wo_a[:, k, :] if k < 8 else wo_sb[:, k - 8, :]
                for n4 in range(4):
                    psb = ps_qk if n4 < 2 else ps_vb
                    ins = e.matmul(psb[:, (n4 % 2) * 512:(n4 % 2 + 1) * 512],
                                   ysb[:, k, tt * 128:(tt + 1) * 128],
                                   wsrc[:, n4 * 512:(n4 + 1) * 512],
                                   start=(k == 0), stop=(k == 15))
            return ins
        t_at = p.op("T", mm_at, waits=y_ld + wo_ld + [r_att], inc="T", pe_w=(0, 1, 2, 3))

        cmb = []
        for n4 in range(4):
            psb = ps_qk if n4 < 2 else ps_vb

            def c1(e, n4=n4, par=par, psb=psb):
                return e.tensor_tensor(outsb[:, par, n4 * 512:(n4 + 1) * 512],
                                       psb[:, (n4 % 2) * 512:(n4 % 2 + 1) * 512],
                                       ffsb[:, par, n4 * 512:(n4 + 1) * 512], ALU.add)
            cmb.append(p.op("V", c1, waits=[t_at, t_ff2, r_outsb[par]], inc="V", va_r=(0, 1, 2, 3)))
        t_c2 = p.op("V", lambda e, par=par: e.tensor_tensor(
            outsb[:, par], outsb[:, par], xr[:, par], ALU.add), waits=cmb + [t_x], inc="V")
        r_att = [t_c2] + cmb
        r_xr[par] = t_c2
        r_ffsb[par] = cmb

        t_st = p.op("Y", lambda e, tt=tt, par=par: e.dma_start(
            out=out[tt * 128:(tt + 1) * 128, :], in_=outsb[:, par]),
            waits=[t_c2], inc="dma_st%d" % par)
        out_stores.append(t_st)
        r_outsb[par] = t_st

    pC.close()
    pWO.close()

    if trunc is not None:
        p.limit = p.marks[trunc] if isinstance(trunc, str) else trunc
    sem_stack = contextlib.ExitStack()
    sems = {}
    for name in sorted(set(list(p.counts.keys()) + ["cc", "dma_fil"])):
        sems[name] = sem_stack.enter_context(nc.semaphore("s_" + name))
    with nc.Block() as block:

        @block.sync
        def _(y):
            p.emit("Y", y, sems)
            if p.limit is not None:
                y.dma_start(out=out, in_=x_own).then_inc(sems["dma_fil"], 16)
                y.wait_ge(sems["dma_fil"], 16)
            else:
                for name in sorted(p.counts):
                    if name.startswith("dma_st"):
                        y.wait_ge(sems[name], p.counts[name])

        @block.tensor
        def _(t):
            p.emit("T", t, sems)

        @block.vector
        def _(v):
            p.emit("V", v, sems)

        @block.scalar
        def _(a):
            p.emit("A", a, sems)

        @block.gpsimd
        def _(g):
            with allow_ici_all_to_all():
                p.emit("P", g, sems)

    sem_stack.close()
    global _LAST_MARKS, _LAST_PROG
    _LAST_MARKS = dict(p.marks)
    _LAST_PROG = p
    root.close()
    return nc


_LAST_MARKS = {}
_LAST_PROG = None
_NC_CACHE = [None]


def make_in_maps(L0_attn_x, L0_attn_Wq, L0_attn_Wk, L0_attn_Wv, L0_attn_Wg, L0_attn_bg,
                 L0_attn_Wo, L0_attn_bo, L0_ffn_w_gate, L0_ffn_w_up, L0_ffn_w_down):
    x = np.ascontiguousarray(np.asarray(L0_attn_x, dtype=np.float32)).reshape(NTOK, DM)
    Wq = np.asarray(L0_attn_Wq, dtype=np.float32)
    Wk = np.asarray(L0_attn_Wk, dtype=np.float32)
    Wv = np.asarray(L0_attn_Wv, dtype=np.float32)
    Wg = np.asarray(L0_attn_Wg, dtype=np.float32)
    bg = np.asarray(L0_attn_bg, dtype=np.float32)
    Wo = np.ascontiguousarray(np.asarray(L0_attn_Wo, dtype=np.float32))
    bo = np.asarray(L0_attn_bo, dtype=np.float32)
    wgate = np.asarray(L0_ffn_w_gate, dtype=np.float32)
    wup = np.asarray(L0_ffn_w_up, dtype=np.float32)
    wdown = np.asarray(L0_ffn_w_down, dtype=np.float32)

    half = 64
    inv_freq = 1.0 / (10000.0 ** (np.arange(half, dtype=np.float64) / half))
    tpos = np.arange(L, dtype=np.float64)
    freqs = tpos[:, None] * inv_freq[None, :]
    cos_t = np.ascontiguousarray(np.cos(freqs).astype(np.float32))
    sin_t = np.sin(freqs).astype(np.float32)
    sin2_t = np.ascontiguousarray(np.concatenate([-sin_t, sin_t], axis=1))
    tri_c = np.ascontiguousarray(np.triu(np.ones((128, 128), np.float32)))
    ident_c = np.ascontiguousarray(np.eye(128, dtype=np.float32))
    ones_c = np.ones((128, 2), np.float32)
    bo_t = np.ascontiguousarray(bo.reshape(16, 128).T)

    wg_t3 = np.ascontiguousarray(
        wgate.reshape(FT, 128, 16, 128).transpose(0, 3, 2, 1).reshape(FT, 128, 16 * 128))
    wu_t3 = np.ascontiguousarray(
        wup.reshape(FT, 128, 16, 128).transpose(0, 3, 2, 1).reshape(FT, 128, 16 * 128))
    wd_t3 = np.ascontiguousarray(
        wdown.reshape(16, 128, FT, 128).transpose(0, 3, 2, 1).reshape(16, 128, FT * 128))

    in_maps = []
    for c in range(NCORES):
        cols = slice(256 * c, 256 * (c + 1))
        wqkv = np.concatenate([Wq[:, cols], Wk[:, cols], Wv[:, cols],
                               Wg[:, 2 * c:2 * c + 2]], axis=1)
        wqkv_t = np.ascontiguousarray(
            wqkv.reshape(16, 128, 770).transpose(1, 0, 2).reshape(128, 16 * 770))
        in_maps.append({
            "x_own": np.ascontiguousarray(x[MYTOK * c:MYTOK * (c + 1)]),
            "wqkv_t": wqkv_t,
            "bg2": np.ascontiguousarray(bg[2 * c:2 * c + 2]),
            "wo": Wo,
            "bo_t": bo_t,
            "wg_t3": wg_t3,
            "wu_t3": wu_t3,
            "wd_t3": wd_t3,
            "cos_t": cos_t,
            "sin2_t": sin2_t,
            "tri_c": tri_c,
            "ident_c": ident_c,
            "ones_c": ones_c,
        })

    return in_maps


def kernel(**inputs):
    in_maps = make_in_maps(**inputs)
    nc = _NC_CACHE[0]
    if nc is None:
        nc = build()
        _NC_CACHE[0] = nc
    with allow_ici_all_to_all():
        res = run_bass_kernel_spmd(nc, in_maps, core_ids=list(range(NCORES)))
    full = np.concatenate([np.asarray(res.results[c]["out"]) for c in range(NCORES)], axis=0)
    return full.reshape(B, L, DM).astype(np.float32)




# version 11
# speedup vs baseline: 1.0826x; 1.0327x over previous
import contextlib
import numpy as np
import ml_dtypes
import concourse.bass as bass
import concourse.mybir as mybir
from concourse.bass_utils import run_bass_kernel_spmd
from concourse.aot_env import allow_ici_all_to_all

F32 = mybir.dt.float32
BF16 = mybir.dt.bfloat16
AF = mybir.ActivationFunctionType
ALU = mybir.AluOpType

NCORES = 8
B = 2
L = 4096
NTOK = B * L
DM = 2048
F = 8192
MYTOK = NTOK // NCORES
TC = 512
NCH = L // TC
FT = F // 128
EPS_LN = 1e-5
EPS_AT = 1e-6
LB_MIN = float(np.log(np.float64(0.8)))
LB_MAX = float(np.log(np.float64(0.9995)))
P2_TILES = 5
NEUTER_YE = False
PIPELINE_P3 = False


def _flat(waits):
    o = []
    for w in waits:
        if w is None:
            continue
        if isinstance(w, list):
            o.extend(x for x in w if x is not None)
        else:
            o.append(w)
    return o


class Prog:
    ENGS = ("T", "V", "A", "P", "Y")

    def __init__(self):
        self.ops = []
        self.counts = {}
        self.last = {}
        self.marks = {}
        self.limit = None
        self.meta = []

    def mark(self, name):
        self.marks[name] = len(self.ops)

    @staticmethod
    def _amt(sem):
        return 16 if sem.startswith("dma") else 1

    def op(self, eng, emit, waits=(), inc=None, pe_w=(), va_r=()):
        waits = tuple(_flat(waits))
        tok = None
        if inc is not None:
            self.counts[inc] = self.counts.get(inc, 0) + self._amt(inc)
            tok = (inc, self.counts[inc])
            self.last[eng] = tok
        self.ops.append((eng, emit, waits, tok))
        self.meta.append((tuple(pe_w), tuple(va_r)))
        return tok

    def snapshot(self):
        return [self.last[e] for e in self.ENGS if e in self.last]

    def emit(self, eng_name, eng_obj, sems):
        hi = {}
        ops = self.ops if self.limit is None else self.ops[:self.limit]
        for eng, emit, waits, tok in ops:
            if eng != eng_name:
                continue
            need = {}
            for (s, v) in waits:
                if need.get(s, -1) < v:
                    need[s] = v
            for s, v in need.items():
                if hi.get(s, -1) < v:
                    eng_obj.wait_ge(sems[s], v)
                    hi[s] = v
            ins = emit(eng_obj)
            if tok is not None:
                ins.then_inc(sems[tok[0]], self._amt(tok[0]))


def build(trunc=None):
    nc = bass.Bass("TRN2", target_bir_lowering=False)

    x_own = nc.dram_tensor("x_own", [MYTOK, DM], F32, kind="ExternalInput").ap()
    wqkv_h = nc.dram_tensor("wqkv_t", [128, 16 * 770], F32, kind="ExternalInput").ap()
    bg2 = nc.dram_tensor("bg2", [2], F32, kind="ExternalInput").ap()
    wo_h = nc.dram_tensor("wo", [DM, DM], F32, kind="ExternalInput").ap()
    bo_h = nc.dram_tensor("bo_t", [128, 16], F32, kind="ExternalInput").ap()
    wg_h = nc.dram_tensor("wg_t3", [FT, 128, 16 * 128], F32, kind="ExternalInput").ap()
    wu_h = nc.dram_tensor("wu_t3", [FT, 128, 16 * 128], F32, kind="ExternalInput").ap()
    wd_h = nc.dram_tensor("wd_t3", [16, 128, FT * 128], F32, kind="ExternalInput").ap()
    cos_h = nc.dram_tensor("cos_t", [L, 64], F32, kind="ExternalInput").ap()
    sin2_h = nc.dram_tensor("sin2_t", [L, 128], F32, kind="ExternalInput").ap()
    tri_h = nc.dram_tensor("tri_c", [128, 128], F32, kind="ExternalInput").ap()
    id_h = nc.dram_tensor("ident_c", [128, 128], F32, kind="ExternalInput").ap()
    ones_h = nc.dram_tensor("ones_c", [128, 2], F32, kind="ExternalInput").ap()

    out = nc.dram_tensor("out", [MYTOK, DM], F32, kind="ExternalOutput").ap()

    xn_loc = nc.dram_tensor("xn_loc", [DM, MYTOK], BF16)
    xn_full_h = nc.dram_tensor("xn_full", [NCORES * DM, MYTOK], BF16, addr_space="Shared")
    xn_full = xn_full_h.ap()
    y_loc_h = nc.dram_tensor("y_loc", [NCORES * 256, MYTOK], BF16)
    y_loc = y_loc_h.ap()
    y_t2_h = nc.dram_tensor("y_t2", [NCORES * 256, MYTOK], BF16)
    y_t2 = y_t2_h.ap()
    ffn_tok = nc.dram_tensor("ffn_tok", [MYTOK, DM], F32)

    p = Prog()
    root = contextlib.ExitStack()

    def sbuf(name, shape, dtype, st=None, side=None):
        kw = {"side": side} if side else {}
        return (st or root).enter_context(nc.sbuf_tensor(name, shape, dtype, **kw))

    ps_qk = root.enter_context(nc.psum_tensor("ps_qk", [128, 1024], F32))
    ps_vb = root.enter_context(nc.psum_tensor("ps_vb", [128, 1024], F32))
    ps_sc = root.enter_context(nc.psum_tensor("ps_sc", [128, 1024], F32))
    ps_ds = root.enter_context(nc.psum_tensor("ps_ds", [128, 512], F32))
    ps_t = root.enter_context(nc.psum_tensor("ps_t", [128, 1024], BF16))

    tri_sb = sbuf("tri_sb", [128, 128], F32)
    id_bf = sbuf("id_bf", [128, 128], BF16)
    ones_bf = sbuf("ones_bf", [128, 2], BF16)
    ones_f32 = sbuf("ones_f32", [128, 1], F32)
    ones_row = sbuf("ones_row", [1, 128], F32)
    bo_sb = sbuf("bo_sb", [128, 16], F32)
    nbg_sb = sbuf("nbg_sb", [128, 2], F32)
    bg_row = sbuf("bg_row", [1, 2], F32)
    gated_a = sbuf("gated_a", [128, P2_TILES, MYTOK], BF16)
    pXN = contextlib.ExitStack()
    xnT_own = sbuf("xnT_own", [128, 16, MYTOK], BF16, pXN, side="right")
    wgu_sb = sbuf("wgu_sb", [128, 2, 2, 16, 128], BF16, pXN, side="right")
    sg_sb = sbuf("sg_sb", [128, 1024], F32, pXN, side="right")
    pA = contextlib.ExitStack()
    wqkv_sb = sbuf("wqkv_sb", [128, 16, 770], BF16, pA)

    t_tri = p.op("Y", lambda e: e.dma_start(out=tri_sb[:], in_=tri_h), inc="dma_tri")
    t_bo = p.op("Y", lambda e: e.dma_start(out=bo_sb[:], in_=bo_h), inc="dma_bo")
    t_idb = p.op("P", lambda e: e.dma_start(out=id_bf[:], in_=id_h), inc="dma_idb")
    t_one = p.op("P", lambda e: e.dma_start(out=ones_bf[:], in_=ones_h), inc="dma_one")
    t_o1 = p.op("V", lambda e: e.memset(ones_f32[:], 1.0), inc="V")
    t_or = p.op("V", lambda e: e.memset(ones_row[:], 1.0), inc="V")
    t_bg = p.op("Y", lambda e: e.dma_start(out=bg_row[:], in_=bg2.unsqueeze(0)),
                inc="dma_bg")
    t_pb = p.op("T", lambda e: e.matmul(ps_ds[:, 0:2], ones_row[:], bg_row[:],
                                        start=True, stop=True), waits=[t_bg, t_or], inc="T", pe_w=(6,))
    t_nbg = p.op("V", lambda e: e.tensor_scalar(nbg_sb[:], ps_ds[:, 0:2], -1.0, None, ALU.mult),
                 waits=[t_pb], inc="V", va_r=(6,))
    t_wqkv = p.op("P", lambda e: e.dma_start(
        out=wqkv_sb[:].rearrange("p a b -> p (a b)"), in_=wqkv_h), inc="dma_qkv")

    pP1 = contextlib.ExitStack()
    xf = sbuf("xf", [128, 2, DM], F32, pP1)
    st6 = sbuf("st6", [128, 24], F32, pP1)
    agg = sbuf("agg", [128, 2], F32, pP1)
    rstd = sbuf("rstd", [128, 1], F32, pP1)
    nmr = sbuf("nmr", [128, 1], F32, pP1)
    xn_tok = sbuf("xn_tok", [128, 2, DM], BF16, pP1)

    r_xf = [None, None]
    r_tpchain = None
    prev_apply = None
    for t in range(8):
        pb = t % 2
        tok_ld = p.op("Y", lambda e, t=t, pb=pb: e.dma_start(
            out=xf[:, pb], in_=x_own[t * 128:(t + 1) * 128, :]), waits=[r_xf[pb]],
            inc="dma_x%d" % pb)

        def stats(e, pb=pb):
            for i in range(4):
                ins = e.bn_stats(st6[:, i * 6:(i + 1) * 6], xf[:, pb, i * 512:(i + 1) * 512])
            return ins
        tok_st = p.op("V", stats, waits=[tok_ld, prev_apply], inc="V")
        tok_ag = p.op("V", lambda e: e.bn_aggr(agg[:], st6[:]), waits=[tok_st], inc="V")
        tok_r0 = p.op("V", lambda e: e.tensor_scalar(rstd[:], agg[:, 1:2], EPS_LN, None, ALU.add),
                      waits=[tok_ag], inc="V")
        tok_r1 = p.op("A", lambda e: e.activation(rstd[:], rstd[:], AF.Ln), waits=[tok_r0], inc="A")
        tok_r2 = p.op("A", lambda e: e.activation(rstd[:], rstd[:], AF.Exp, scale=-0.5),
                      waits=[tok_r1], inc="A")
        tok_nm = p.op("V", lambda e: e.tensor_scalar(nmr[:], agg[:, 0:1], rstd[:], -1.0,
                                                     ALU.mult, ALU.mult), waits=[tok_r2], inc="V")
        tok_ap = p.op("A", lambda e, pb=pb: e.activation(
            xn_tok[:, pb], xf[:, pb], AF.Identity, bias=nmr[:], scale=rstd[:]),
            waits=[tok_nm], inc="A")
        prev_apply = tok_ap
        r_xf[pb] = tok_ap

        for g in range(2):
            def tr(e, pb=pb, g=g):
                for kk in range(8):
                    ins = e.transpose(ps_t[:, kk * 128:(kk + 1) * 128],
                                      xn_tok[:, pb, (g * 8 + kk) * 128:(g * 8 + kk + 1) * 128],
                                      id_bf[:])
                return ins
            tok_tr = p.op("T", tr, waits=[tok_ap, r_tpchain, t_idb], inc="T", pe_w=(7,))

            def ev(e, t=t, g=g):
                return e.tensor_copy(
                    xnT_own[:, g * 8:(g + 1) * 8, t * 128:(t + 1) * 128],
                    ps_t[:].rearrange("p (a b) -> p a b", a=8, b=128))
            r_tpchain = p.op("V", ev, waits=[tok_tr], inc="V", va_r=(7,))

    xn_done = [r_tpchain]
    ag_in = []
    for k in range(16):
        ag_in.append(p.op("Y", lambda e, k=k: e.dma_start(
            out=xn_loc[k * 128:(k + 1) * 128, :], in_=xnT_own[:, k, :]),
            waits=xn_done, inc="dma_xnl"))
    p.op("P", lambda e: e.collective_compute(
        "AllGather", ALU.bypass, replica_groups=[list(range(NCORES))],
        ins=[xn_loc.ap().opt()], outs=[xn_full_h.ap().opt()]), waits=ag_in, inc="cc")

    st_ffn = {"wgu": [None, None], "sg": None, "up_ps": None}

    def ffn_a_tile(ff, gated_dst, extra_waits=()):
        dbuf = ff % 2
        w1 = p.op("P", lambda e, ff=ff, dbuf=dbuf: e.dma_start(
            out=wgu_sb[:, 0, dbuf].rearrange("p a b -> p (a b)"), in_=wg_h[ff]),
            waits=[st_ffn["wgu"][dbuf]] + list(extra_waits), inc="dma_wg%d" % dbuf)
        w2 = p.op("P", lambda e, ff=ff, dbuf=dbuf: e.dma_start(
            out=wgu_sb[:, 1, dbuf].rearrange("p a b -> p (a b)"), in_=wu_h[ff]),
            waits=[st_ffn["wgu"][dbuf]], inc="dma_wu%d" % dbuf)

        def mm_gate(e, dbuf=dbuf):
            for k in range(16):
                for h2 in range(2):
                    ins = e.matmul(ps_qk[:, h2 * 512:(h2 + 1) * 512], wgu_sb[:, 0, dbuf, k],
                                   xnT_own[:, k, h2 * 512:(h2 + 1) * 512],
                                   start=(k == 0), stop=(k == 15))
            return ins
        tg = p.op("T", mm_gate,
                  waits=[w1, st_ffn["sg"]] + xn_done + list(extra_waits), inc="T", pe_w=(0, 1))

        def mm_up(e, dbuf=dbuf):
            for k in range(16):
                for h2 in range(2):
                    ins = e.matmul(ps_vb[:, h2 * 512:(h2 + 1) * 512], wgu_sb[:, 1, dbuf, k],
                                   xnT_own[:, k, h2 * 512:(h2 + 1) * 512],
                                   start=(k == 0), stop=(k == 15))
            return ins
        tu = p.op("T", mm_up, waits=[w2, st_ffn["up_ps"]] + list(extra_waits), inc="T", pe_w=(2, 3))

        tsg = p.op("A", lambda e: e.activation(sg_sb[:], ps_qk[:], AF.Silu),
                   waits=[tg, st_ffn["up_ps"]], inc="A", va_r=(0, 1))
        st_ffn["sg"] = tsg

        tgm = p.op("V", lambda e, gated_dst=gated_dst: e.tensor_tensor(
            gated_dst, sg_sb[:], ps_vb[:], ALU.mult), waits=[tsg, tu], inc="V", va_r=(2, 3))
        st_ffn["up_ps"] = tgm
        st_ffn["wgu"][dbuf] = tgm
        return tgm

    gated_toks = []
    for ff in range(P2_TILES):
        gated_toks.append(ffn_a_tile(ff, gated_a[:, ff, :]))

    pP1.close()
    xnT_st = sbuf("xnT_st", [128, 2, 2, 16, TC], BF16, pA)
    ropeC = sbuf("ropeC", [128, 4, 64], F32, pA)
    ropeS = sbuf("ropeS", [128, 4, 128], F32, pA)
    qkf = sbuf("qkf", [128, 8, 512], F32, pA)
    phis = sbuf("phis", [128, 2, 512], F32, pA)
    qkt = sbuf("qkt", [128, 8, 512], BF16, pA)
    qkfT = sbuf("qkfT", [128, 8, 512], BF16, pA)
    vsb = sbuf("vsb", [128, 8, 2, 132], BF16, pA)
    aw_sb = sbuf("aw_sb", [128, 2, 128], BF16, pA)
    ssb = sbuf("ssb", [128, 4, 132], F32, pA)
    ssb_bf = sbuf("ssb_bf", [128, 4, 132], BF16, pA)
    y_tokb = sbuf("y_tokb", [128, 16, 128], BF16, pA)
    yT_sb = sbuf("yT_sb", [128, 2, 2, 512], BF16, pA)
    bsb = sbuf("bsb", [128, 2, 8], F32, pA)
    ebs = sbuf("ebs", [128, 2, 8], F32, pA)
    lb_sb = sbuf("lb_sb", [128, 2, 8], F32, pA)
    psc_sb = sbuf("psc_sb", [128, 4, 8], F32, pA)
    plr_sb = sbuf("plr_sb", [1, 16], F32, pA)
    plast_sb = sbuf("plast_sb", [128, 2, 8], F32, pA)
    den_sb = sbuf("den_sb", [128, 2, 1], F32, pA)

    t_vones = p.op("V", lambda e: e.memset(
        vsb[:].rearrange("p a b c -> p (a b c)"), 1.0), inc="V")

    S = dict(
        r_xnT=[None, None], r_rope=None,
        r_qk=[st_ffn["sg"], st_ffn["sg"]], r_vb=[st_ffn["up_ps"], st_ffn["up_ps"]],
        r_sc=[[], []], r_dsl=[[t_vones], [t_vones]], r_clog=[t_nbg], r_psc=[],
        r_tpchain=xn_done[0], r_qkf=None, r_phis=None,
        r_qkt=None, r_qkfT=None, r_vsb=t_vones,
        r_aw=[None, None], r_ytok=None, r_yT=None,
        r_plr=None, r_plbc=t_nbg, r_den=[None, None], r_bsb1=None,
        state_tok=[None, None, None, None], state_bf_tok=[None, None, None, None],
        g_cnt=0,
    )
    y_stores = []

    def p3_proj(ci, gi):
        ctx = {"sl": gi % 2, "ci": ci, "gi": gi, "psum_reads": []}
        sl = gi % 2
        l0 = ci * TC
        w_first = [("cc", 1)] if gi == 0 else []
        xn_toks = []
        for b in range(2):
            t0 = b * L + ci * TC
            rank, col0 = divmod(t0, MYTOK)
            for kq in range(2):
                xn_toks.append(p.op("Y", lambda e, rank=rank, col0=col0, sl=sl, b=b, kq=kq:
                    e.dma_start(
                        out=xnT_st[:, sl, b, kq * 8:(kq + 1) * 8],
                        in_=xn_full[rank * DM + kq * 1024: rank * DM + (kq + 1) * 1024,
                                    col0:col0 + TC].rearrange("(k p) n -> p k n", p=128)),
                    waits=[S["r_xnT"][sl]] + w_first, inc="dma_xn%d" % sl))
        tok_rc = p.op("Y", lambda e, l0=l0: e.dma_start(
            out=ropeC[:],
            in_=cos_h[l0:l0 + TC, :].rearrange("(s p) f -> p s f", p=128)),
            waits=[S["r_rope"]], inc="dma_rc0")
        tok_rs2 = p.op("Y", lambda e, l0=l0: e.dma_start(
            out=ropeS[:],
            in_=sin2_h[l0:l0 + TC, :].rearrange("(s p) f -> p s f", p=128)),
            waits=[S["r_rope"]], inc="dma_rs0")

        ctx.update(xn_toks=xn_toks, tok_rc=tok_rc, tok_rs2=tok_rs2)
        vb_toks = []
        for s8 in range(8):
            b, s = divmod(s8, 4)
            pb = s8 % 2

            def mm_vb(e, b=b, s=s, sl=sl, pb=pb):
                for k in range(16):
                    ins = e.matmul(ps_vb[:, pb * 512:pb * 512 + 258],
                                   xnT_st[:, sl, b, k, s * 128:(s + 1) * 128],
                                   wqkv_sb[:, k, 512:770], start=(k == 0), stop=(k == 15))
                return ins
            t_mm = p.op("T", mm_vb, waits=xn_toks + [t_wqkv, S["r_vb"][pb], S["r_plbc"]],
                        inc="T", pe_w=(2 + pb,))

            t_evv = p.op("A", lambda e, s8=s8, sl=sl, pb=pb: e.copy(
                vsb[:, s8, :, 0:128],
                ps_vb[:, pb * 512:pb * 512 + 256].rearrange("p (h d) -> p h d", h=2, d=128)),
                waits=[t_mm, S["r_vsb"]], inc="A", va_r=(2 + pb,))
            t_cb = p.op("V", lambda e, s8=s8, sl=sl, pb=pb: e.tensor_copy(
                bsb[:, :, s8], ps_vb[:, pb * 512 + 256:pb * 512 + 258]),
                waits=[t_mm, S["r_bsb1"]], inc="V", va_r=(2 + pb,))
            S["r_vb"][pb] = [t_evv, t_cb]
            vb_toks.append([t_evv, t_cb])
            ctx["psum_reads"].extend([t_evv, t_cb])
        ctx["vb_toks"] = vb_toks

        return ctx

    def p3_mid(ctx):
        sl = ctx["sl"]
        w_exp = _flat(ctx["vb_toks"]) + [t_nbg]
        eb_toks = []
        for h in range(2):
            eb_toks.append(p.op("A", lambda e, h=h, sl=sl: e.activation(
                ebs[:, h, :], bsb[:, h, :], AF.Exp, scale=-1.0, bias=nbg_sb[:, h:h + 1]),
                waits=w_exp + [S["r_ebs"]] if "r_ebs" in S else w_exp, inc="A"))
        S["r_bsb1"] = eb_toks
        t_t2 = p.op("V", lambda e: e.tensor_scalar(
            ebs[:].rearrange("p a b -> p (a b)"), ebs[:].rearrange("p a b -> p (a b)"),
            1.0, None, ALU.add), waits=eb_toks, inc="V")
        t_ln = p.op("A", lambda e: e.activation(
            lb_sb[:].rearrange("p a b -> p (a b)"), ebs[:].rearrange("p a b -> p (a b)"),
            AF.Ln), waits=[t_t2] + S.get("r_lb", []), inc="A")
        S["r_ebs"] = t_ln
        t_lb1 = p.op("V", lambda e: e.tensor_scalar(
            lb_sb[:].rearrange("p a b -> p (a b)"), lb_sb[:].rearrange("p a b -> p (a b)"),
            -1.0, LB_MAX, ALU.mult, ALU.min), waits=[t_ln], inc="V")
        t_lb = p.op("V", lambda e: e.tensor_scalar(
            lb_sb[:].rearrange("p a b -> p (a b)"), lb_sb[:].rearrange("p a b -> p (a b)"),
            LB_MIN, None, ALU.max), waits=[t_lb1], inc="V")

        t_clog = p.op("T", lambda e: e.matmul(
            ps_ds[:, 264:280], tri_sb[:], lb_sb[:].rearrange("p a b -> p (a b)"),
            start=True, stop=True),
            waits=[t_lb, t_tri] + S["r_clog"] + S["r_dsl"][0] + S["r_dsl"][1], inc="T",
            pe_w=(6,))
        t_sum = p.op("T", lambda e: e.matmul(
            ps_ds[0:1, 280:296], ones_f32[:], lb_sb[:].rearrange("p a b -> p (a b)"),
            start=True, stop=True), waits=[t_lb, t_o1], inc="T", pe_w=(6,))
        S["r_lb"] = [t_clog, t_sum]
        t_p1 = p.op("A", lambda e: e.activation(
            psc_sb[:, 0:2, :].rearrange("p a b -> p (a b)"), ps_ds[:, 264:280], AF.Exp),
            waits=[t_clog, t_sum] + S["r_psc"], inc="A", va_r=(6,))
        t_p2 = p.op("A", lambda e: e.activation(
            psc_sb[:, 2:4, :].rearrange("p a b -> p (a b)"), ps_ds[:, 264:280], AF.Exp,
            scale=-1.0), waits=[t_clog, t_sum] + S["r_psc"], inc="A", va_r=(6,))
        t_pe = p.op("V", lambda e: e.tensor_copy(plr_sb[:], ps_ds[0:1, 280:296]),
                    waits=[t_sum, S["r_plr"]], inc="V", va_r=(6,))
        t_pbc = p.op("T", lambda e: e.matmul(
            ps_ds[:, 296:312], ones_row[:], plr_sb[:], start=True, stop=True),
            waits=[t_pe, t_p1, t_p2, S["r_plbc"]], inc="T", pe_w=(6,))
        S["r_plr"] = t_pbc
        t_pl = p.op("A", lambda e: e.activation(
            plast_sb[:].rearrange("p a b -> p (a b)"), ps_ds[:, 296:312], AF.Exp),
            waits=[t_pbc] + S["state_tok"], inc="A", va_r=(6,))
        S["r_plbc"] = t_pl
        S["r_clog"] = [t_pl, t_pe]
        ctx.update(t_p1=t_p1, t_p2=t_p2, t_pe=t_pe, t_pl=t_pl)
        return ctx

    def p3_qk(ctx):
        sl = ctx["sl"]
        xn_toks = ctx["xn_toks"]
        tok_rc = ctx["tok_rc"]
        tok_rs2 = ctx["tok_rs2"]
        ctx["f5s"] = {}
        for s8 in range(8):
            b, s = divmod(s8, 4)
            pb = s8 % 2

            def mm_qk(e, b=b, s=s, sl=sl, pb=pb):
                for k in range(16):
                    ins = e.matmul(ps_qk[:, pb * 512:(pb + 1) * 512],
                                   xnT_st[:, sl, b, k, s * 128:(s + 1) * 128],
                                   wqkv_sb[:, k, 0:512], start=(k == 0), stop=(k == 15))
                return ins
            t_mmqk = p.op("T", mm_qk, waits=xn_toks + [t_wqkv, S["r_qk"][pb]], inc="T",
                          pe_w=(pb,))

            def rope1(e, s=s, sl=sl, pb=pb):
                src2 = ps_qk[:, pb * 512:(pb + 1) * 512].rearrange(
                    "p (g two f) -> p g two f", g=4, two=2, f=64)
                sw = src2[:, :, ::-1, :]
                sinb = ropeS[:, s].rearrange("p (two f) -> p two f", two=2) \
                    .unsqueeze(1).broadcast_to((128, 4, 2, 64))
                return e.tensor_tensor(
                    phis[:, 0].rearrange("p (g two f) -> p g two f", g=4, two=2, f=64),
                    sw, sinb, ALU.mult)
            t_r1 = p.op("V", rope1, waits=[t_mmqk, tok_rs2, S["r_phis"]], inc="V", va_r=(pb,))

            def rope2(e, s=s, sl=sl, pb=pb):
                src2 = ps_qk[:, pb * 512:(pb + 1) * 512].rearrange(
                    "p (g two f) -> p g two f", g=4, two=2, f=64)
                cosb = ropeC[:, s].unsqueeze(1).unsqueeze(2).broadcast_to((128, 4, 2, 64))
                return e.tensor_tensor(
                    phis[:, 1].rearrange("p (g two f) -> p g two f", g=4, two=2, f=64),
                    src2, cosb, ALU.mult)
            t_r2 = p.op("V", rope2, waits=[t_mmqk, tok_rc, S["r_phis"]], inc="V", va_r=(pb,))
            S["r_qk"][pb] = [t_r1, t_r2]
            ctx["psum_reads"].extend([t_r1, t_r2])
            S["r_rope"] = t_r2

            t_r3 = p.op("V", lambda e, s8=s8: e.tensor_tensor(
                qkf[:, s8], phis[:, 0], phis[:, 1], ALU.add),
                waits=[t_r1, t_r2, S["r_qkf"]], inc="V")
            S["r_phis"] = t_r3
            S["r_xnT"][sl] = t_mmqk

            t_f1 = p.op("V", lambda e, s8=s8: e.tensor_scalar(
                phis[:, 0], qkf[:, s8], 0.0, None, ALU.min),
                waits=[t_r3, S["r_phis"]], inc="V")
            t_f2 = p.op("A", lambda e: e.activation(phis[:, 1], phis[:, 0], AF.Exp),
                        waits=[t_f1], inc="A")

            def f3(e, s8=s8):
                pv = psc_sb[:, :, s8].unsqueeze(2).broadcast_to((128, 4, 128))
                return e.scalar_tensor_tensor(
                    phis[:, 0].rearrange("p (g f) -> p g f", g=4, f=128),
                    qkf[:, s8].rearrange("p (g f) -> p g f", g=4, f=128),
                    1.0, pv, ALU.add, ALU.mult)
            t_f3 = p.op("V", f3, waits=[t_f1, t_f2, ctx["t_p1"], ctx["t_p2"]], inc="V")
            S["r_qkf"] = t_f3

            def f4(e, s8=s8):
                pv = psc_sb[:, :, s8].unsqueeze(2).broadcast_to((128, 4, 128))
                return e.tensor_tensor(
                    phis[:, 1].rearrange("p (g f) -> p g f", g=4, f=128),
                    phis[:, 1].rearrange("p (g f) -> p g f", g=4, f=128),
                    pv, ALU.mult)
            t_f4 = p.op("V", f4, waits=[t_f2, ctx["t_p1"], ctx["t_p2"]], inc="V")

            t_f5 = p.op("V", lambda e, s8=s8: e.tensor_tensor(
                qkt[:, s8], phis[:, 0], phis[:, 1], ALU.max),
                waits=[t_f3, t_f4, S["r_qkt"]], inc="V")
            S["r_phis"] = t_f5
            ctx["f5s"][s8] = t_f5
            ctx["t_f34l"] = [t_f3, t_f4]
        return ctx

    def p3_scan(ctx, guard=()):
        sl = ctx["sl"]
        ci = ctx["ci"]
        vb_toks = ctx["vb_toks"]
        t_p1, t_p2, t_pe, t_pl = ctx["t_p1"], ctx["t_p2"], ctx["t_pe"], ctx["t_pl"]
        if ci == 0:
            t_rs = p.op("V", lambda e: e.memset(ssb[:].rearrange("p a b -> p (a b)"), 0.0),
                        waits=S["state_tok"] + S["state_bf_tok"], inc="V")
            t_rsb = p.op("V", lambda e: e.memset(ssb_bf[:].rearrange("p a b -> p (a b)"), 0.0),
                         waits=S["state_tok"] + S["state_bf_tok"], inc="V")
            S["state_tok"] = [t_rs] * 4
            S["state_bf_tok"] = [t_rsb] * 4

        ye_toks = {0: [], 1: []}
        t_ds_last = None
        order = [0, 4, 1, 5, 2, 6, 3, 7]
        for oi, s8 in enumerate(order):
            b, s = divmod(s8, 4)
            t_f5 = ctx["f5s"][s8]

            def trq(e, sl=sl, s8=s8):
                for g in range(4):
                    ins = e.transpose(ps_t[:, g * 128:(g + 1) * 128],
                                      qkt[:, s8, g * 128:(g + 1) * 128], id_bf[:])
                return ins
            t_tr = p.op("T", trq, waits=[t_f5, S["r_tpchain"], t_idb] +
                        list(guard if oi == 0 else ()), inc="T", pe_w=(7,))
            t_evq = p.op("V", lambda e, sl=sl, s8=s8: e.tensor_copy(
                qkfT[:, s8], ps_t[:, 0:512]),
                waits=[t_tr, S["r_qkfT"]], inc="V", va_r=(7,))
            S["r_tpchain"] = t_evq

            for h in range(2):
                chain = b * 2 + h
                gpar = S["g_cnt"] % 2
                bank = ps_sc[:, gpar * 512:(gpar + 1) * 512]
                dslot = S["g_cnt"] % 2
                dsl_ap = ps_ds[:, dslot * 132:dslot * 132 + 129]
                qf = qkfT[:, s8, h * 128:(h + 1) * 128]
                kf = qkfT[:, s8, 256 + h * 128:256 + h * 128 + 128]
                kt = qkt[:, s8, 256 + h * 128:256 + h * 128 + 128]
                vv = vsb[:, s8, h]

                t_A = p.op("T", lambda e, bank=bank, kf=kf, qf=qf: e.matmul(
                    bank[:, 0:128], kf, qf, start=True, stop=True),
                    waits=[t_evq] + S["r_sc"][gpar], inc="T", pe_w=(4 + gpar,))
                t_aw = p.op("V", lambda e, bank=bank, gpar=gpar: e.tensor_tensor(
                    aw_sb[:, gpar], bank[:, 0:128], tri_sb[:], ALU.mult),
                    waits=[t_A, S["r_aw"][gpar], t_tri], inc="V", va_r=(4 + gpar,))
                t_y1 = p.op("T", lambda e, bank=bank, gpar=gpar, vv=vv: e.matmul(
                    bank[:, 128:256], aw_sb[:, gpar], vv[:, 0:128], start=True, stop=False),
                    waits=[t_aw] + vb_toks[s8], inc="T", pe_w=(4 + gpar,))
                t_y2 = p.op("T", lambda e, bank=bank, qf=qf, chain=chain: e.matmul(
                    bank[:, 128:256], qf, ssb_bf[:, chain, 0:128], start=False, stop=True),
                    waits=[S["state_bf_tok"][chain]], inc="T", pe_w=(4 + gpar,))
                t_d1 = p.op("T", lambda e, bank=bank, gpar=gpar: e.matmul(
                    bank[:, 256:257], aw_sb[:, gpar], ones_bf[:, 0:1],
                    start=True, stop=False), waits=[t_aw, t_one], inc="T", pe_w=(4 + gpar,))
                t_d2 = p.op("T", lambda e, bank=bank, qf=qf, chain=chain: e.matmul(
                    bank[:, 256:257], qf, ssb_bf[:, chain, 128:129], start=False, stop=True),
                    waits=[S["state_bf_tok"][chain]], inc="T", pe_w=(4 + gpar,))
                t_ds = p.op("T", lambda e, dsl_ap=dsl_ap, kt=kt, vv=vv: e.matmul(
                    dsl_ap, kt, vv[:, 0:129], start=True, stop=True),
                    waits=[t_f5, t_p1, t_p2, t_pe, t_pl] + S["r_dsl"][0] + S["r_dsl"][1]
                    + vb_toks[s8], inc="T", pe_w=(6,))
                t_ds_last = t_ds

                t_de = p.op("V", lambda e, bank=bank, gpar=gpar: e.tensor_scalar(
                    den_sb[:, gpar], bank[:, 256:257], EPS_AT, None, ALU.add),
                    waits=[t_d2, S["r_den"][gpar]], inc="V", va_r=(4 + gpar,))
                t_dr = p.op("V", lambda e, gpar=gpar: e.reciprocal(
                    den_sb[:, gpar], den_sb[:, gpar]), waits=[t_de], inc="V")
                yslot = s8 * 2 + h
                t_ye = p.op("A", lambda e, bank=bank, gpar=gpar, yslot=yslot: e.activation(
                    y_tokb[:, yslot], bank[:, 128:256], AF.Identity, scale=den_sb[:, gpar]),
                    waits=[t_y2, t_dr, S["r_ytok"]], inc="A", va_r=(4 + gpar,))
                S["r_den"][gpar] = t_ye
                S["r_sc"][gpar] = [t_aw, t_ye, t_dr]
                S["r_aw"][gpar] = [t_y1, t_d1]
                ye_toks[b].append(t_ye)

                t_s1 = p.op("V", lambda e, chain=chain, dsl_ap=dsl_ap: e.tensor_tensor(
                    ssb[:, chain, 0:129], ssb[:, chain, 0:129], dsl_ap, ALU.add),
                    waits=[t_ds, S["state_tok"][chain]], inc="V", va_r=(6,))
                t_s2 = p.op("V", lambda e, chain=chain, h=h, s8=s8: e.tensor_scalar(
                    ssb[:, chain, 0:129], ssb[:, chain, 0:129], plast_sb[:, h, s8:s8 + 1],
                    None, ALU.mult), waits=[t_s1, t_pl], inc="V")
                S["state_tok"][chain] = t_s2
                S["r_dsl"][dslot] = [t_s1]
                t_sb2 = p.op("A", lambda e, chain=chain: e.copy(
                    ssb_bf[:, chain, 0:129], ssb[:, chain, 0:129]),
                    waits=[t_s2, t_y2, t_d2], inc="A")
                S["state_bf_tok"][chain] = t_sb2
                S["g_cnt"] += 1

        S["r_vsb"] = S["state_bf_tok"][3]
        S["r_qkt"] = S["state_bf_tok"][3]
        S["r_qkfT"] = S["state_bf_tok"][3]
        S["r_psc"] = ctx["t_f34l"] + [t_ds_last]

        for b in range(2):
            def tr_y(e, b=b):
                for j in range(8):
                    ins = e.transpose(ps_t[:, j * 128:(j + 1) * 128],
                                      y_tokb[:, b * 8 + j], id_bf[:])
                return ins
            t_yT = p.op("T", tr_y, waits=ye_toks[b] + [S["r_tpchain"]], inc="T", pe_w=(7,))

            def ev_y(e, b=b):
                src_v = ps_t[:].rearrange("p (s2 h d) -> p s2 h d", s2=4, h=2, d=128) \
                    .rearrange("p s2 h d -> p h s2 d")
                return e.tensor_copy(
                    yT_sb[:, b].rearrange("p h (s2 d) -> p h s2 d", s2=4, d=128), src_v)
            t_yTe = p.op("V", ev_y, waits=[t_yT, S["r_yT"]], inc="V", va_r=(7,))
            S["r_tpchain"] = t_yTe
            S["r_ytok"] = t_yT

            t0b = b * L + ci * TC
            rank, col0 = divmod(t0b, MYTOK)
            for h in range(2):
                y_stores.append(p.op("Y", lambda e, rank=rank, col0=col0, b=b, h=h:
                    e.dma_start(
                        out=y_loc[rank * 256 + h * 128: rank * 256 + (h + 1) * 128,
                                  col0:col0 + TC],
                        in_=yT_sb[:, b, h, :]), waits=[t_yTe], inc="dma_ys0"))
            S["r_yT"] = y_stores[-1]

    p.mark("preP3")
    for gi, ci in enumerate(range(NCH)):
        ctx = p3_proj(ci, gi)
        p3_mid(ctx)
        p3_qk(ctx)
        p3_scan(ctx)
        p.mark("chunk%d" % gi)
    p.mark("P3done")

    r_qk = S["r_qk"]
    r_vb = S["r_vb"]
    r_sc = S["r_sc"]
    r_dsl = S["r_dsl"]
    r_clog = S["r_clog"]

    p3_done = p.snapshot()
    p.op("P", lambda e: e.collective_compute(
        "AllToAll", ALU.bypass, replica_groups=[list(range(NCORES))],
        ins=[y_loc_h.ap().opt()], outs=[y_t2_h.ap().opt()]),
        waits=y_stores + [("cc", 1)], inc="cc")

    p.mark("A2A")
    p3_ps_free = _flat([r_qk[0], r_qk[1], r_vb[0], r_vb[1]])
    pA.close()

    pB = contextlib.ExitStack()
    gated_b = sbuf("gated_b", [128, FT - P2_TILES, MYTOK], BF16, pB)
    for ff in range(P2_TILES, FT):
        extra = (p3_ps_free + p3_done) if ff == P2_TILES else ()
        gated_toks.append(ffn_a_tile(ff, gated_b[:, ff - P2_TILES, :], extra))
    gated_all = gated_toks[-1]

    def gated_get(f):
        if f < P2_TILES:
            return gated_a[:, f, :]
        return gated_b[:, f - P2_TILES, :]

    p.mark("P4a")
    pXN.close()
    pWO = contextlib.ExitStack()
    wo_a = sbuf("wo_a", [128, 8, DM], BF16, pWO, side="right")
    id_f32 = sbuf("id_f32", [128, 128], F32, pB)
    t_idf = p.op("Y", lambda e: e.dma_start(out=id_f32[:], in_=id_h),
                 waits=p3_done, inc="dma_idf")
    wd_sb = sbuf("wd_sb", [128, 2, FT, 128], BF16, pB)
    ffn_f = sbuf("ffn_f", [128, 2, 512], F32, pB)
    fftT = sbuf("fftT", [128, 2, 512], F32, pB)

    r_wd = [[gated_all] + p3_done, [gated_all] + p3_done]
    r_dn = [r_sc[0], r_sc[1]]
    r_ff = [None, None]
    r_fftT = [None, None]
    r_dstr = [r_dsl[0] + r_dsl[1] + r_clog, None]
    last_dn = None
    ffn_stores = []
    dn_cnt = 0
    for dout in range(16):
        dbuf = dout % 2
        t_wd = p.op("P", lambda e, dout=dout, dbuf=dbuf: e.dma_start(
            out=wd_sb[:, dbuf].rearrange("p a b -> p (a b)"), in_=wd_h[dout]),
            waits=[r_wd[dbuf]], inc="dma_wd%d" % dbuf)
        for half in range(2):
            par = dn_cnt % 2
            bank = ps_sc[:, par * 512:(par + 1) * 512]

            def mm_dn(e, dbuf=dbuf, half=half, bank=bank):
                for f in range(FT):
                    ins = e.matmul(bank[:, 0:512], wd_sb[:, dbuf, f],
                                   gated_get(f)[:, half * 512:(half + 1) * 512],
                                   start=(f == 0), stop=(f == FT - 1))
                return ins
            t_dn = p.op("T", mm_dn, waits=[t_wd, gated_all, r_dn[par]], inc="T", pe_w=(4 + par,))
            last_dn = t_dn

            t_fe = p.op("A", lambda e, par=par, bank=bank, dout=dout: e.activation(
                ffn_f[:, par], bank[:, 0:512], AF.Identity, bias=bo_sb[:, dout:dout + 1]),
                waits=[t_dn, r_ff[par], t_bo], inc="A", va_r=(4 + par,))
            r_dn[par] = t_fe

            def trf(e, par=par):
                for i in range(4):
                    ins = e.transpose(ps_ds[:, i * 128:(i + 1) * 128],
                                      ffn_f[:, par, i * 128:(i + 1) * 128], id_f32[:])
                return ins
            t_tf = p.op("T", trf, waits=[t_fe, r_dstr[0], r_dstr[1], t_idf], inc="T", pe_w=(6,))
            r_ff[par] = t_tf

            t_te = p.op("V", lambda e, par=par: e.tensor_copy(fftT[:, par], ps_ds[:, 0:512]),
                        waits=[t_tf, r_fftT[par]], inc="V", va_r=(6,))
            r_dstr = [t_te, None]

            t_sf = p.op("Y", lambda e, par=par, half=half, dout=dout: e.dma_start(
                out=ffn_tok[half * 512:(half + 1) * 512, dout * 128:(dout + 1) * 128]
                .rearrange("(i p) c -> p i c", p=128),
                in_=fftT[:, par].rearrange("p (i c) -> p i c", i=4, c=128)),
                waits=[t_te], inc="dma_ff%d" % par)
            ffn_stores.append(t_sf)
            r_fftT[par] = t_sf
            dn_cnt += 1
        r_wd[dbuf] = last_dn

    p.mark("P4b")
    wo_a_ld = []
    for k in range(8):
        wo_a_ld.append(p.op("P", lambda e, k=k: e.dma_start(
            out=wo_a[:, k, :], in_=wo_h[k * 128:(k + 1) * 128, :]),
            waits=[gated_all] + p3_done, inc="dma_wo"))

    p4_done = p.snapshot()
    pB.close()
    pC = contextlib.ExitStack()
    ysb = sbuf("ysb", [128, 16, MYTOK], BF16, pC)
    wo_sb = sbuf("wo_sb", [128, 8, DM], BF16, pC)
    xr = sbuf("xr", [128, 2, DM], F32, pC)
    ffsb = sbuf("ffsb", [128, 2, DM], F32, pC)
    outsb = sbuf("outsb", [128, 2, DM], F32, pC)

    y_ld = []
    for k in range(16):
        y_ld.append(p.op("Y", lambda e, k=k: e.dma_start(
            out=ysb[:, k, :], in_=y_t2[k * 128:(k + 1) * 128, :]),
            waits=[("cc", 2)] + p4_done, inc="dma_y2"))
    wo_ld = list(wo_a_ld)
    for k in range(8, 16):
        wo_ld.append(p.op("P", lambda e, k=k: e.dma_start(
            out=wo_sb[:, k - 8, :], in_=wo_h[k * 128:(k + 1) * 128, :]),
            waits=p4_done, inc="dma_wo"))

    r_xr = [p4_done, p4_done]
    r_ffsb = [p4_done, p4_done]
    r_outsb = [None, None]
    r_att = [[st_ffn["sg"], st_ffn["up_ps"]] for _ in range(4)]
    out_stores = []
    for tt in range(8):
        par = tt % 2
        t_x = p.op("Y", lambda e, tt=tt, par=par: e.dma_start(
            out=xr[:, par], in_=x_own[tt * 128:(tt + 1) * 128, :]),
            waits=[r_xr[par]], inc="dma_xr%d" % par)
        t_ff2 = p.op("Y", lambda e, tt=tt, par=par: e.dma_start(
            out=ffsb[:, par], in_=ffn_tok[tt * 128:(tt + 1) * 128, :]),
            waits=[r_ffsb[par]] + (ffn_stores if tt < 2 else []), inc="dma_fb%d" % par)

        cmb = []
        for n4 in range(4):
            psb = ps_qk if n4 < 2 else ps_vb

            def mm_at(e, tt=tt, n4=n4, psb=psb):
                for k in range(16):
                    wsrc = wo_a[:, k, :] if k < 8 else wo_sb[:, k - 8, :]
                    ins = e.matmul(psb[:, (n4 % 2) * 512:(n4 % 2 + 1) * 512],
                                   ysb[:, k, tt * 128:(tt + 1) * 128],
                                   wsrc[:, n4 * 512:(n4 + 1) * 512],
                                   start=(k == 0), stop=(k == 15))
                return ins
            t_at = p.op("T", mm_at, waits=y_ld + wo_ld + [r_att[n4]], inc="T", pe_w=(n4,))

            def c1(e, n4=n4, par=par, psb=psb, t_at=t_at):
                return e.tensor_tensor(outsb[:, par, n4 * 512:(n4 + 1) * 512],
                                       psb[:, (n4 % 2) * 512:(n4 % 2 + 1) * 512],
                                       ffsb[:, par, n4 * 512:(n4 + 1) * 512], ALU.add)
            cmb.append(p.op("V", c1, waits=[t_at, t_ff2, r_outsb[par]], inc="V",
                            va_r=(n4,)))
            r_att[n4] = cmb[-1]
        t_c2 = p.op("V", lambda e, par=par: e.tensor_tensor(
            outsb[:, par], outsb[:, par], xr[:, par], ALU.add), waits=cmb + [t_x], inc="V")
        r_xr[par] = t_c2
        r_ffsb[par] = cmb

        t_st = p.op("Y", lambda e, tt=tt, par=par: e.dma_start(
            out=out[tt * 128:(tt + 1) * 128, :], in_=outsb[:, par]),
            waits=[t_c2], inc="dma_st%d" % par)
        out_stores.append(t_st)
        r_outsb[par] = t_st

    pC.close()
    pWO.close()

    if trunc is not None:
        p.limit = p.marks[trunc] if isinstance(trunc, str) else trunc
    sem_stack = contextlib.ExitStack()
    sems = {}
    for name in sorted(set(list(p.counts.keys()) + ["cc", "dma_fil"])):
        sems[name] = sem_stack.enter_context(nc.semaphore("s_" + name))
    with nc.Block() as block:

        @block.sync
        def _(y):
            p.emit("Y", y, sems)
            if p.limit is not None:
                y.dma_start(out=out, in_=x_own).then_inc(sems["dma_fil"], 16)
                y.wait_ge(sems["dma_fil"], 16)
            else:
                for name in sorted(p.counts):
                    if name.startswith("dma_st"):
                        y.wait_ge(sems[name], p.counts[name])

        @block.tensor
        def _(t):
            p.emit("T", t, sems)

        @block.vector
        def _(v):
            p.emit("V", v, sems)

        @block.scalar
        def _(a):
            p.emit("A", a, sems)

        @block.gpsimd
        def _(g):
            with allow_ici_all_to_all():
                p.emit("P", g, sems)

    sem_stack.close()
    global _LAST_MARKS, _LAST_PROG
    _LAST_MARKS = dict(p.marks)
    _LAST_PROG = p
    root.close()
    return nc


_LAST_MARKS = {}
_LAST_PROG = None
_NC_CACHE = [None]


def make_in_maps(L0_attn_x, L0_attn_Wq, L0_attn_Wk, L0_attn_Wv, L0_attn_Wg, L0_attn_bg,
                 L0_attn_Wo, L0_attn_bo, L0_ffn_w_gate, L0_ffn_w_up, L0_ffn_w_down):
    x = np.ascontiguousarray(np.asarray(L0_attn_x, dtype=np.float32)).reshape(NTOK, DM)
    Wq = np.asarray(L0_attn_Wq, dtype=np.float32)
    Wk = np.asarray(L0_attn_Wk, dtype=np.float32)
    Wv = np.asarray(L0_attn_Wv, dtype=np.float32)
    Wg = np.asarray(L0_attn_Wg, dtype=np.float32)
    bg = np.asarray(L0_attn_bg, dtype=np.float32)
    Wo = np.ascontiguousarray(np.asarray(L0_attn_Wo, dtype=np.float32))
    bo = np.asarray(L0_attn_bo, dtype=np.float32)
    wgate = np.asarray(L0_ffn_w_gate, dtype=np.float32)
    wup = np.asarray(L0_ffn_w_up, dtype=np.float32)
    wdown = np.asarray(L0_ffn_w_down, dtype=np.float32)

    half = 64
    inv_freq = 1.0 / (10000.0 ** (np.arange(half, dtype=np.float64) / half))
    tpos = np.arange(L, dtype=np.float64)
    freqs = tpos[:, None] * inv_freq[None, :]
    cos_t = np.ascontiguousarray(np.cos(freqs).astype(np.float32))
    sin_t = np.sin(freqs).astype(np.float32)
    sin2_t = np.ascontiguousarray(np.concatenate([-sin_t, sin_t], axis=1))
    tri_c = np.ascontiguousarray(np.triu(np.ones((128, 128), np.float32)))
    ident_c = np.ascontiguousarray(np.eye(128, dtype=np.float32))
    ones_c = np.ones((128, 2), np.float32)
    bo_t = np.ascontiguousarray(bo.reshape(16, 128).T)

    wg_t3 = np.ascontiguousarray(
        wgate.reshape(FT, 128, 16, 128).transpose(0, 3, 2, 1).reshape(FT, 128, 16 * 128))
    wu_t3 = np.ascontiguousarray(
        wup.reshape(FT, 128, 16, 128).transpose(0, 3, 2, 1).reshape(FT, 128, 16 * 128))
    wd_t3 = np.ascontiguousarray(
        wdown.reshape(16, 128, FT, 128).transpose(0, 3, 2, 1).reshape(16, 128, FT * 128))

    in_maps = []
    for c in range(NCORES):
        cols = slice(256 * c, 256 * (c + 1))
        wqkv = np.concatenate([Wq[:, cols], Wk[:, cols], Wv[:, cols],
                               Wg[:, 2 * c:2 * c + 2]], axis=1)
        wqkv_t = np.ascontiguousarray(
            wqkv.reshape(16, 128, 770).transpose(1, 0, 2).reshape(128, 16 * 770))
        in_maps.append({
            "x_own": np.ascontiguousarray(x[MYTOK * c:MYTOK * (c + 1)]),
            "wqkv_t": wqkv_t,
            "bg2": np.ascontiguousarray(bg[2 * c:2 * c + 2]),
            "wo": Wo,
            "bo_t": bo_t,
            "wg_t3": wg_t3,
            "wu_t3": wu_t3,
            "wd_t3": wd_t3,
            "cos_t": cos_t,
            "sin2_t": sin2_t,
            "tri_c": tri_c,
            "ident_c": ident_c,
            "ones_c": ones_c,
        })

    return in_maps


def kernel(**inputs):
    in_maps = make_in_maps(**inputs)
    nc = _NC_CACHE[0]
    if nc is None:
        nc = build()
        _NC_CACHE[0] = nc
    with allow_ici_all_to_all():
        res = run_bass_kernel_spmd(nc, in_maps, core_ids=list(range(NCORES)))
    full = np.concatenate([np.asarray(res.results[c]["out"]) for c in range(NCORES)], axis=0)
    return full.reshape(B, L, DM).astype(np.float32)


